# version 12
# baseline (speedup 1.0000x reference)
import numpy as np
import ml_dtypes
from contextlib import ExitStack

import concourse.bass as bass
import concourse.mybir as mybir
from concourse.bass_utils import run_bass_kernel_spmd

F32 = mybir.dt.float32
BF16 = mybir.dt.bfloat16
AF = mybir.ActivationFunctionType
ALU = mybir.AluOpType
AX = mybir.AxisListType
AP = bass.AP

NCORES = 8
L = 4096
D = 2048
PROJ = 4224
NPO = 33
HBCT = 17
OC = 512
NOC = L // OC
FT = 512
NFP = L // FT
NH = 16
DH = 128
SCALE_QK = float(1.0 / np.sqrt(np.sqrt(128.0)))

ENGS = ("sync", "tensor", "vector", "scalar", "gpsimd")


SERIALIZE_DMA = [False]


class Rec:

    def __init__(self, nc, sems, cnt):
        self.nc = nc
        self.sems = sems
        self.cnt = cnt
        self.ops = {e: [] for e in ENGS}
        self.last_dma = {}

    def wait(self, eng, sem, val=None):
        v = self.cnt[sem] if val is None else val
        if v <= 0:
            return
        h = self.sems[sem]
        self.ops[eng].append(lambda e, h=h, v=v: e.wait_ge(h, v))

    def op(self, eng, fn, inc=None, n=1):
        if inc is not None:
            self.cnt[inc] += n
            h = self.sems[inc]
            self.ops[eng].append(lambda e, f=fn, h=h, n=n: f(e).then_inc(h, n))
            return self.cnt[inc]
        self.ops[eng].append(fn)
        return None

    def dma(self, eng, out, in_, inc=None, **kw):
        if inc is None:
            inc = {"sync": "dW", "gpsimd": "dG", "scalar": "dS",
                   "vector": "dV", "tensor": "dT"}[eng]
        if SERIALIZE_DMA[0] and eng in self.last_dma:
            ls, lv = self.last_dma[eng]
            self.wait(eng, ls, lv)
        self.cnt[inc] += 16
        self.last_dma[eng] = (inc, self.cnt[inc])
        h = self.sems[inc]
        self.ops[eng].append(
            lambda e, o=out, i=in_, h=h, kw=kw: e.dma_start(o, i, **kw).then_inc(h, 16))
        return self.cnt[inc]

    def fence(self):
        snap = [(s, self.cnt[s]) for s in self.sems]
        for e in ENGS:
            for s, v in snap:
                self.wait(e, s, v)

    def emit(self):
        with self.nc.Block() as blk:
            for e in ENGS:
                ops = self.ops[e]
                if ops:
                    getattr(blk, e)(lambda eng, ops=ops: [f(eng) for f in ops])
        self.ops = {e: [] for e in ENGS}



def build_nc(upto="all", debug=(), extern=(), m1dbg=False):
    nc = bass.Bass("TRN2", target_bir_lowering=False,
                   detect_race_conditions=False)

    di = {}

    def inp(name, shape, dt=F32):
        di[name] = nc.dram_tensor(name, list(shape), dt, kind="ExternalInput")
        return di[name]

    inp("x", (L, D))
    inp("W_in", (D, PROJ))
    inp("b_in_cols", (128, NPO))
    inp("conv_w_cols", (128, HBCT * 4))
    inp("conv_b_cols", (128, HBCT))
    inp("A_col", (32, 1))
    inp("dtb_col", (32, 1))
    inp("D_cols", (128, 16))
    inp("norm_cols", (128, 16))
    inp("W_out", (D, D))
    inp("Wq", (D, D)); inp("Wk", (D, D)); inp("Wv", (D, D)); inp("Wo", (D, D))
    inp("bq_cols", (128, 16)); inp("bk_cols", (128, 16))
    inp("bv_row", (1, D)); inp("bo_row", (1, D))
    inp("w_gate", (D, 8192)); inp("w_up", (D, 8192)); inp("w_down", (8192, D))
    inp("ident_bf", (128, 128), BF16)
    inp("ident_f32", (128, 128))
    inp("triu_bf", (128, 128), BF16)
    inp("neg_ones", (1, 2048))

    dr = {}
    def idram(name, shape, dt):
        k = "ExternalOutput" if name in extern else None
        if k:
            return nc.dram_tensor(name, list(shape), dt, kind=k)
        return nc.dram_tensor(name, list(shape), dt)
    dr["xbuf"] = idram("xbuf", [L, D], F32)
    dr["winb"] = idram("winb", [NPO * 16 * 128, 128], BF16)
    dr["woutb"] = idram("woutb", [D, D], BF16)
    dr["wqTb"] = idram("wqTb", [D, D], BF16)
    dr["wkTb"] = idram("wkTb", [D, D], BF16)
    dr["wvTb"] = idram("wvTb", [D, D], BF16)
    dr["woTb"] = idram("woTb", [D, D], BF16)
    dr["wgb"] = idram("wgb", [64 * 16 * 128, 128], BF16)
    dr["wub"] = idram("wub", [64 * 16 * 128, 128], BF16)
    dr["wdb"] = idram("wdb", [8192, D], BF16)
    dr["ynormb"] = idram("ynormb", [D, L], BF16)
    dr["qb_d"] = idram("qb_d", [D, L], BF16)
    dr["kb_d"] = idram("kb_d", [D, L], BF16)
    dr["vb_d"] = idram("vb_d", [NH * L, DH], BF16)
    dr["attnb"] = idram("attnb", [D, L], BF16)
    dr["bounce"] = idram("bounce", [64, 4096], F32)
    dr["bounce_bf"] = idram("bounce_bf", [16, 4096], BF16)
    if m1dbg:
        dr["mdbg"] = nc.dram_tensor("mdbg", [128, 8192], F32,
                                    kind="ExternalOutput")
    dr["out"] = nc.dram_tensor("out", [L, D], F32, kind="ExternalOutput")
    dbg = {}
    for name in debug:
        dbg[name] = nc.dram_tensor("dbg_" + name, [L, D], F32,
                                   kind="ExternalOutput")

    sem_names = ["dW", "dG", "dS", "dV", "dT", "tE", "vE", "sE", "gE",
                 "g0", "g1", "g2", "g3", "g4", "g5", "g6", "g7", "g8", "g9",
                 "gA", "gB", "gC", "gD"]

    with ExitStack() as st:
        sems = {s: st.enter_context(nc.semaphore(s)) for s in sem_names}
        cnt = {s: 0 for s in sem_names}

        C = {}

        def sb(name, shape, dt=F32):
            C[name] = st.enter_context(nc.sbuf_tensor("c_" + name, list(shape), dt))
            return C[name]

        sb("ident_b", (128, 128), BF16)
        sb("ident_f", (128, 128))
        sb("triu_b", (128, 128), BF16)
        sb("ones_cb", (128, 1), BF16)
        sb("ones_cf", (128, 1))
        sb("ones_rb", (1, 512), BF16)
        sb("ones_rf", (1, 512))
        sb("eps_ln", (128, 1))
        sb("eps_rms", (1, 1))
        sb("binc", (128, NPO))
        sb("cwc", (128, HBCT * 4))
        sb("cbc", (128, HBCT))
        sb("a_col", (32, 1))
        sb("dtb_col", (32, 1))
        sb("d_cols", (128, 16))
        sb("bq_c", (128, 16))
        sb("bk_c", (128, 16))
        sb("bv_r", (1, D), BF16)
        sb("bo_r", (1, D), BF16)

        P = dict(nc=nc, sems=sems, cnt=cnt, di=di, dr=dr, dbg=dbg, C=C,
                 m1dbg=m1dbg)

        phase_prep(P)
        if upto != "prep":
            phase_m1(P)
            phase_m2(P)
            dump_x(P, "x1")
        if upto in ("f1", "attn", "all"):
            phase_ffn(P, src=dr["xbuf"], dst=dr["xbuf"])
            dump_x(P, "x2")
        if upto in ("attn", "all"):
            phase_a1(P)
            phase_a2(P)
            phase_a3(P)
            dump_x(P, "x3")
        if upto == "all":
            phase_ffn(P, src=dr["xbuf"], dst=dr["out"])
        R = Rec(nc, sems, cnt)
        R.fence()
        R.emit()
    return nc


def dump_x(P, name):
    if name not in P["dbg"]:
        return
    R = Rec(P["nc"], P["sems"], P["cnt"])
    R.fence()
    for i in range(8):
        R.dma("gpsimd", P["dbg"][name][i * 512:(i + 1) * 512, :],
              P["dr"]["xbuf"][i * 512:(i + 1) * 512, :])
    R.fence()
    R.emit()


def emit_ln(R, P, xtile, lnt, lnsq, stat, glpos, dsem="dG"):
    C = P["C"]
    cnt = P["cnt"]
    R.wait("vector", dsem, glpos)
    R.op("vector", lambda e: e.reduce_sum(stat[:, 0:1], xtile[:, :], AX.X),
         inc="vE")
    R.op("vector", lambda e: e.tensor_mul(lnsq[:, :], xtile[:, :], xtile[:, :]))
    R.op("vector", lambda e: e.reduce_sum(stat[:, 1:2], lnsq[:, :], AX.X),
         inc="vE")
    R.wait("vector", "vE", cnt["vE"])
    R.op("vector", lambda e: e.tensor_scalar(
        stat[:, 2:3], stat[:, 0:1], 1.0 / D, None, ALU.mult), inc="vE")
    R.wait("vector", "vE", cnt["vE"])
    R.op("vector", lambda e: e.tensor_mul(stat[:, 3:4], stat[:, 2:3],
                                          stat[:, 2:3]))
    R.op("vector", lambda e: e.tensor_scalar(
        stat[:, 4:5], stat[:, 1:2], 1.0 / D, None, ALU.mult), inc="vE")
    R.wait("vector", "vE", cnt["vE"])
    R.op("vector", lambda e: e.tensor_sub(stat[:, 5:6], stat[:, 4:5],
                                          stat[:, 3:4]), inc="vE")
    R.wait("scalar", "vE", cnt["vE"])
    R.op("scalar", lambda e: e.activation(
        stat[:, 6:7], stat[:, 5:6], AF.Ln, bias=C["eps_ln"][:, :]))
    R.op("scalar", lambda e: e.activation(
        stat[:, 7:8], stat[:, 6:7], AF.Exp, scale=-0.5), inc="sE")
    R.wait("vector", "sE", cnt["sE"])
    R.op("vector", lambda e: e.tensor_scalar(
        lnsq[:, :], xtile[:, :], stat[:, 2:3], None, ALU.subtract))
    return R.op("vector", lambda e: e.tensor_scalar(
        lnt[:, :], lnsq[:, :], stat[:, 7:8], None, ALU.mult), inc="vE")


def emit_ln_transpose(R, P, lnt, lncm, psum, tt, width, nslots=4):
    cnt = P["cnt"]
    idf = P["C"]["ident_f"]
    R.wait("tensor", "vE", cnt["vE"])
    for ct in range(16):
        pp = ct % nslots
        R.op("tensor", lambda e, ct=ct, pp=pp, psum=psum: e.transpose(
            psum[:, pp * 512:pp * 512 + 128],
            lnt[:, ct * 128:(ct + 1) * 128], idf[:, :]), inc="tE")
        R.wait("scalar", "tE", cnt["tE"])
        R.op("scalar", lambda e, ct=ct, pp=pp, psum=psum: e.copy(
            lncm[:, ct * width + tt * 128:ct * width + (tt + 1) * 128],
            psum[:, pp * 512:pp * 512 + 128]), inc="sE")
        if ct >= nslots - 1:
            R.wait("tensor", "sE", cnt["sE"] - (nslots - 1))
    return cnt["sE"]



def phase_prep(P):
    nc, sems, cnt, di, dr, C = (P["nc"], P["sems"], P["cnt"], P["di"], P["dr"],
                                P["C"])
    R = Rec(nc, sems, cnt)
    with ExitStack() as st:
        wt = [st.enter_context(nc.sbuf_tensor("p_wt%d" % i, [128, 2048], F32))
              for i in range(4)]
        wb = [st.enter_context(nc.sbuf_tensor("p_wb%d" % i, [128, 2048], BF16))
              for i in range(4)]
        pb = st.enter_context(nc.sbuf_tensor("p_pb", [128, 8192], BF16))
        tb = st.enter_context(nc.sbuf_tensor("p_tb", [128, 2048], BF16))
        ncol = st.enter_context(nc.sbuf_tensor("p_ncol", [128, 16], F32))
        ps = st.enter_context(nc.psum_tensor("p_ps", [128, 2048], BF16))

        for nm, dst in (("ident_bf", "ident_b"), ("ident_f32", "ident_f"),
                        ("triu_bf", "triu_b"), ("b_in_cols", "binc"),
                        ("conv_w_cols", "cwc"), ("conv_b_cols", "cbc"),
                        ("dtb_col", "dtb_col"), ("D_cols", "d_cols"),
                        ("A_col", "a_col"), ("bq_cols", "bq_c"),
                        ("bk_cols", "bk_c")):
            R.dma("gpsimd", C[dst][:, :], di[nm][:, :])
        R.dma("gpsimd", ncol[:, :], di["norm_cols"][:, :])
        R.dma("gpsimd", wt[0][0:1, 0:D], di["bv_row"][:, :])
        g_loads = R.dma("gpsimd", wt[1][0:1, 0:D], di["bo_row"][:, :])
        for nm, v in (("ones_cb", 1.0), ("ones_cf", 1.0), ("ones_rb", 1.0),
                      ("ones_rf", 1.0), ("eps_ln", 1e-5), ("eps_rms", 1e-6)):
            R.op("vector", lambda e, nm=nm, v=v: e.memset(C[nm][:, :], v))
        R.wait("vector", "dG", g_loads)
        R.op("vector", lambda e: e.tensor_copy(C["bv_r"][:, :], wt[0][0:1, 0:D]))
        R.op("vector", lambda e: e.tensor_copy(C["bo_r"][:, :], wt[1][0:1, 0:D]),
             inc="vE")
        bvbo_v = cnt["vE"]
        R.wait("scalar", "dG", g_loads)
        R.op("scalar", lambda e: e.activation(C["a_col"][:, :], C["a_col"][:, :],
                                              AF.Exp))
        R.op("scalar", lambda e: e.mul(C["a_col"][:, :], C["a_col"][:, :], -1.0))
        R.op("scalar", lambda e: e.mul(C["bq_c"][:, :], C["bq_c"][:, :], SCALE_QK))
        R.op("scalar", lambda e: e.mul(C["bk_c"][:, :], C["bk_c"][:, :], SCALE_QK),
             inc="sE")

        state = {"i": 0, "wt_free": [bvbo_v, bvbo_v, 0, 0],
                 "wb_free": [0, 0, 0, 0],
                 "wt_sem": ["vE", "vE", "vE", "vE"]}
        STMAP = ["g0", "g1", "g4", "g5"]

        def cast_block(src_ap, scale=None, nw_ct=None, ceng="vector"):
            i = state["i"]; state["i"] += 1
            p = i % 4
            qeng = "sync" if p % 2 == 0 else "scalar"
            R.wait(qeng, state["wt_sem"][p], state["wt_free"][p])
            wl = R.dma(qeng, wt[p][:, :], src_ap, inc="g%d" % (6 + p))
            csem = "vE" if ceng == "vector" else "sE"
            R.wait(ceng, "g%d" % (6 + p), wl)
            R.wait(ceng, STMAP[p], state["wb_free"][p])
            if nw_ct is not None:
                v = R.op(ceng, lambda e, p=p, c=nw_ct: e.tensor_scalar(
                    wb[p][:, :], wt[p][:, :], ncol[:, c:c + 1], None, ALU.mult),
                    inc=csem)
            elif scale is not None:
                v = R.op(ceng, lambda e, p=p, s=scale: e.tensor_scalar(
                    wb[p][:, :], wt[p][:, :], float(s), None, ALU.mult),
                    inc=csem)
            elif ceng == "vector":
                v = R.op(ceng, lambda e, p=p: e.tensor_copy(
                    wb[p][:, :], wt[p][:, :]), inc=csem)
            else:
                v = R.op(ceng, lambda e, p=p: e.activation(
                    wb[p][:, :], wt[p][:, :], AF.Identity), inc=csem)
            state["wt_free"][p] = v
            state["wt_sem"][p] = csem
            return p, csem, v

        for j in range(NPO):
            src = AP(di["W_in"], j * 128,
                     [[PROJ, 128], [128 * PROJ, 16], [1, 128]])
            p, csem, v = cast_block(src)
            R.wait("gpsimd", csem, v)
            state["wb_free"][p] = R.dma(
                "gpsimd", dr["winb"][j * 2048:(j + 1) * 2048, :],
                AP(wb[p], 0, [[2048, 128], [128, 16], [1, 128]]),
                inc=STMAP[p])

        for ct in range(16):
            p, csem, v = cast_block(di["W_out"][ct * 128:(ct + 1) * 128, :],
                                    nw_ct=ct)
            R.wait("gpsimd", csem, v)
            state["wb_free"][p] = R.dma(
                "gpsimd", dr["woutb"][ct * 128:(ct + 1) * 128, :], wb[p][:, :],
                inc=STMAP[p])

        tb_free = 0
        prev_ev = {14: ("sE", 0), 15: ("vE", 0)}
        for (wnm, dnm, scale) in (("Wq", "wqTb", SCALE_QK),
                                  ("Wk", "wkTb", SCALE_QK),
                                  ("Wv", "wvTb", None), ("Wo", "woTb", None)):
            for i in range(16):
                p, csem, v = cast_block(di[wnm][i * 128:(i + 1) * 128, :],
                                        scale=scale)
                R.wait("tensor", csem, v)
                ev_done = {}
                for jj in range(16):
                    pp = jj % 2
                    ceng2 = "scalar" if jj % 2 == 0 else "vector"
                    if jj >= 2:
                        R.wait("tensor", ev_done[jj - 2][0], ev_done[jj - 2][1])
                    else:
                        R.wait("tensor", prev_ev[14 + jj][0],
                               prev_ev[14 + jj][1])
                    tpos = R.op(
                        "tensor", lambda e, p=p, jj=jj, pp=pp: e.transpose(
                            ps[:, pp * 1024:pp * 1024 + 128],
                            wb[p][:, jj * 128:(jj + 1) * 128],
                            C["ident_b"][:, :]), inc="tE")
                    R.wait(ceng2, "tE", tpos)
                    if jj <= 1:
                        R.wait(ceng2, "g2", tb_free)
                    if ceng2 == "scalar":
                        c = R.op("scalar", lambda e, jj=jj, pp=pp: e.copy(
                            tb[:, jj * 128:(jj + 1) * 128],
                            ps[:, pp * 1024:pp * 1024 + 128]), inc="sE")
                        ev_done[jj] = ("sE", c)
                    else:
                        c = R.op("vector", lambda e, jj=jj, pp=pp: e.tensor_copy(
                            tb[:, jj * 128:(jj + 1) * 128],
                            ps[:, pp * 1024:pp * 1024 + 128]), inc="vE")
                        ev_done[jj] = ("vE", c)
                state["wb_free"][p] = 0
                prev_ev = {14: ev_done[14], 15: ev_done[15]}
                R.wait("gpsimd", "sE", ev_done[14][1])
                R.wait("gpsimd", "vE", ev_done[15][1])
                dst = AP(dr[dnm], i * 128, [[D, 128], [128 * D, 16], [1, 128]])
                tb_free = R.dma("gpsimd", dst,
                                AP(tb, 0, [[2048, 128], [128, 16], [1, 128]]),
                                inc="g2")
                R.wait("sync", "tE", cnt["tE"])
                R.wait("scalar", "tE", cnt["tE"])

        for i in range(64):
            p, csem, v = cast_block(di["w_down"][i * 128:(i + 1) * 128, :])
            R.wait("gpsimd", csem, v)
            state["wb_free"][p] = R.dma(
                "gpsimd", dr["wdb"][i * 128:(i + 1) * 128, :], wb[p][:, :],
                inc=STMAP[p])

        pb_free = 0
        for (wnm, dnm) in (("w_gate", "wgb"), ("w_up", "wub")):
            for ct in range(16):
                for q in range(4):
                    p, csem, v = cast_block(
                        di[wnm][ct * 128:(ct + 1) * 128,
                                q * 2048:(q + 1) * 2048], ceng="scalar")
                    R.wait("vector", csem, v)
                    if q == 0:
                        R.wait("vector", "g3", pb_free)
                    for cb in range(4):
                        for j in range(4):
                            it = (4 * q + cb) * 4 + j
                            src = AP(wb[p], cb * 512 + j, [[2048, 128], [4, 128]])
                            R.op(
                                "vector", lambda e, src=src, it=it:
                                e.tensor_copy(pb[:, it * 128:(it + 1) * 128],
                                              src),
                                inc=("vE" if (cb == 3 and j == 3) else None))
                    state["wb_free"][p] = 0
                    state["wt_free"][p] = cnt["vE"]
                    state["wt_sem"][p] = "vE"
                R.wait("gpsimd", "vE", cnt["vE"])
                for g4 in range(4):
                    dst = AP(dr[dnm], g4 * 16 * 2048 * 128 + ct * 128,
                             [[2048, 128], [2048 * 128, 16], [1, 128]])
                    src = AP(pb, g4 * 16 * 128,
                             [[8192, 128], [128, 16], [1, 128]])
                    pb_free = R.dma("gpsimd", dst, src, inc="g3")
        R.fence()
        R.emit()



def phase_m1(P):
    nc, sems, cnt, di, dr, C = (P["nc"], P["sems"], P["cnt"], P["di"], P["dr"],
                                P["C"])
    R = Rec(nc, sems, cnt)
    with ExitStack() as st:
        def sba(name, shape, dt=F32):
            return st.enter_context(nc.sbuf_tensor(name, list(shape), dt))
        wsl = [sba("m1_w%d" % i, (128, 2048), BF16) for i in range(2)]
        xt = [sba("m1_x%d" % i, (128, 2048)) for i in range(2)]
        lnt = sba("m1_lnt", (128, 2048))
        lncm = sba("m1_lncm", (128, 16 * OC), BF16)
        gate = sba("m1_gate", (128, 16 * OC), BF16)
        hbc = sba("m1_hbc", (128, HBCT * (OC + 4)), BF16)
        conv = sba("m1_conv", (128, HBCT * OC))
        cacc = sba("m1_cacc", (128, OC))
        dts = sba("m1_dts", (32, OC))
        dtv = sba("m1_dtv", (32, OC))
        sp1 = sba("m1_sp1", (32, OC))
        cum = sba("m1_cum", (32, 128))
        ecum_b = sba("m1_ecb", (32, 128), BF16)
        ecum_fl = sba("m1_ecfl", (1, 2048), BF16)
        aend_v = sba("m1_aev", (32, 1))
        aend_c = sba("m1_aec", (32, 32))
        stagA = sba("m1_stA", (2, 2048))
        stagB = sba("m1_stB", (2, 2048))
        dab = sba("m1_dab", (128, 2048))
        dec = sba("m1_dec", (128, 2048), BF16)
        decm = sba("m1_decm", (128, 2048), BF16)
        mt = sba("m1_mt", (128, 2048), BF16)
        gt_b = sba("m1_gt", (128, 128), BF16)
        cpp = sba("m1_cpp", (32, 2048), BF16)
        xs_tm = sba("m1_xstm", (128, 2048), BF16)
        dt_tm = sba("m1_dttm", (128, 32))
        xp = sba("m1_xp", (128, 2048), BF16)
        xpp = sba("m1_xpp", (128, 2048), BF16)
        btm = sba("m1_btm", (128, 32), BF16)
        cols2 = sba("m1_c2", (128, 32))
        c0 = sba("m1_c0", (32, 128))
        hst = sba("m1_h", (32, 2048))
        hbf = sba("m1_hbf", (32, 2048), BF16)
        ysub = lnt
        ysq = dec
        ynb = xpp
        rstd_r = sba("m1_rsr", (1, 128))
        rstd_bc = sba("m1_rbc", (128, 128))
        stat = sba("m1_stat", (128, 8))
        P1 = st.enter_context(nc.psum_tensor("m1_P1", [128, 2048], F32))
        P2 = st.enter_context(nc.psum_tensor("m1_P2", [128, 2048], F32))

        idf = C["ident_f"]
        triu = C["triu_b"]

        R.op("vector", lambda e: e.memset(hst[:, :], 0.0))
        R.op("vector", lambda e: e.memset(hbf[:, :], 0.0))
        R.op("vector", lambda e: e.memset(stagA[0:1, :], 1.0))
        negones_g = R.dma("gpsimd", stagB[1:2, :], di["neg_ones"][0:1, :])
        for t in range(HBCT):
            R.op("vector", lambda e, t=t: e.memset(
                hbc[:, t * (OC + 4):t * (OC + 4) + 4], 0.0))
        R.op("vector", lambda e: e.memset(
            hbc[64:128, 16 * (OC + 4):17 * (OC + 4)], 0.0), inc="vE")

        xfree = [0, 0]
        ynb_g = [0]
        wfree = [0, 0]
        pevict = [0, 0]
        for oc in range(NOC):
            tok0 = oc * OC
            for tt in range(4):
                p = tt % 2
                R.wait("gpsimd", "vE", xfree[p])
                gl = R.dma("gpsimd", xt[p][:, :],
                           di["x"][tok0 + tt * 128:tok0 + (tt + 1) * 128, :],
                           inc="g%d" % p)
                if P["m1dbg"] and oc == 0 and tt == 0:
                    R.wait("gpsimd", "g0", gl)
                    R.dma("gpsimd", dr["mdbg"][:, 6272:6784], xt[0][:, 0:512])
                    R.wait("vector", "dG", cnt["dG"])
                xfree[p] = emit_ln(R, P, xt[p], lnt, dab, stat, gl,
                                   dsem="g%d" % p)
                if P["m1dbg"] and oc == 0 and tt == 0:
                    R.wait("gpsimd", "vE", xfree[p])
                    R.dma("gpsimd", dr["mdbg"][:, 6794:6802], stat[:, :])
                    R.dma("gpsimd", dr["mdbg"][:, 6802:7314], lnt[:, 0:512])
                    R.wait("vector", "dG", cnt["dG"])
                    R.wait("scalar", "dG", cnt["dG"])
                emit_ln_transpose(R, P, lnt, lncm, P2, tt, OC)
            R.wait("tensor", "sE", cnt["sE"])
            ln_done_s = cnt["sE"]

            if oc > 0:
                for t in range(HBCT):
                    o = t * (OC + 4)
                    R.op("vector", lambda e, o=o: e.tensor_copy(
                        hbc[:, o:o + 4], hbc[:, o + OC:o + OC + 4]),
                        inc=("vE" if t == HBCT - 1 else None))
            halo_v = cnt["vE"]

            p1_v_guard = cnt["vE"]
            for j in range(NPO):
                p = j % 2
                pj = j % 2
                R.wait("sync", "tE", wfree[p])
                wpos = R.dma("sync", wsl[p][:, :],
                             dr["winb"][j * 2048:(j + 1) * 2048, :],
                             inc="g%d" % (6 + p))
                R.wait("tensor", "g%d" % (6 + p), wpos)
                R.wait("tensor", "sE", ln_done_s)
                R.wait("tensor", "sE", pevict[pj])
                if j < 2:
                    R.wait("tensor", "vE", p1_v_guard)
                for ct in range(16):
                    R.op("tensor", lambda e, p=p, ct=ct, pj=pj: e.matmul(
                        P1[:, pj * 512:(pj + 1) * 512],
                        wsl[p][:, ct * 128:(ct + 1) * 128],
                        lncm[:, ct * OC:ct * OC + 512],
                        start=(ct == 0), stop=(ct == 15)),
                        inc=("tE" if ct == 15 else None))
                wfree[p] = cnt["tE"]
                R.wait("scalar", "tE", cnt["tE"])
                R.wait("scalar", "vE", halo_v)
                if j < 16:
                    R.op("scalar", lambda e, j=j, pj=pj: e.activation(
                        gate[:, j * OC:(j + 1) * OC],
                        P1[:, pj * 512:(pj + 1) * 512], AF.Silu,
                        bias=C["binc"][:, j:j + 1]), inc="sE")
                elif j < 32:
                    t = j - 16
                    R.op("scalar", lambda e, t=t, j=j, pj=pj: e.activation(
                        hbc[:, t * (OC + 4) + 4:t * (OC + 4) + 4 + OC],
                        P1[:, pj * 512:(pj + 1) * 512], AF.Identity,
                        bias=C["binc"][:, j:j + 1]), inc="sE")
                else:
                    R.op("scalar", lambda e, pj=pj: e.activation(
                        hbc[0:64, 16 * (OC + 4) + 4:16 * (OC + 4) + 4 + OC],
                        P1[0:64, pj * 512:pj * 512 + OC], AF.Identity,
                        bias=C["binc"][0:64, 32:33]), inc="sE")
                    R.op("scalar", lambda e, pj=pj: e.activation(
                        dts[:, 0:OC], P1[64:96, pj * 512:pj * 512 + OC],
                        AF.Identity, bias=C["binc"][64:96, 32:33]), inc="sE")
                pevict[pj] = cnt["sE"]
            proj_s = cnt["sE"]

            R.wait("vector", "sE", proj_s)
            for t in range(HBCT):
                o = t * (OC + 4)
                R.op("vector", lambda e, t=t, o=o: e.tensor_scalar(
                    cacc[:, :], hbc[:, o + 1:o + 1 + OC],
                    C["cwc"][:, t * 4:t * 4 + 1], None, ALU.mult))
                for k in (1, 2, 3):
                    R.op("vector", lambda e, t=t, o=o, k=k:
                         e.scalar_tensor_tensor(
                             cacc[:, :], hbc[:, o + 1 + k:o + 1 + k + OC],
                             C["cwc"][:, t * 4 + k:t * 4 + k + 1], cacc[:, :],
                             ALU.mult, ALU.add),
                         inc=("vE" if k == 3 else None))
                R.wait("scalar", "vE", cnt["vE"])
                R.op("scalar", lambda e, t=t: e.activation(
                    conv[:, t * OC:(t + 1) * OC], cacc[:, :], AF.Silu,
                    bias=C["cbc"][:, t:t + 1]), inc="sE")
                R.wait("vector", "sE", cnt["sE"])
            conv_s = cnt["sE"]

            R.op("scalar", lambda e: e.activation(
                sp1[:, :], dts[:, :], AF.Abs, bias=C["dtb_col"][:, :]))
            R.op("scalar", lambda e: e.activation(
                sp1[:, :], sp1[:, :], AF.Exp, scale=-1.0))
            R.op("scalar", lambda e: e.activation(
                sp1[:, :], sp1[:, :], AF.Ln, bias=1.0), inc="sE")
            R.wait("vector", "sE", cnt["sE"])
            R.op("vector", lambda e: e.tensor_scalar(
                dtv[:, :], dts[:, :], C["dtb_col"][:, :], None, ALU.add),
                inc="vE")
            R.wait("vector", "vE", cnt["vE"])
            R.op("vector", lambda e: e.tensor_scalar(
                dtv[:, :], dtv[:, :], 0.0, None, ALU.max), inc="vE")
            R.wait("vector", "vE", cnt["vE"])
            R.op("vector", lambda e: e.tensor_add(dtv[:, :], dtv[:, :],
                                                  sp1[:, :]), inc="vE")
            R.wait("vector", "vE", cnt["vE"])
            R.op("vector", lambda e: e.tensor_scalar(
                sp1[:, :], dtv[:, :], C["a_col"][:, :], None, ALU.mult),
                inc="vE")
            dt_v = cnt["vE"]

            for sc in range(4):
                s0 = sc * 128
                gtok = tok0 + s0
                bs = sc % 4
                R.wait("vector", "vE", dt_v)
                R.op("vector", lambda e, s0=s0: e.tensor_tensor_scan(
                    cum[:, :], sp1[:, s0:s0 + 128], sp1[:, s0:s0 + 128],
                    0.0, ALU.add, ALU.bypass), inc="vE")
                R.wait("scalar", "vE", cnt["vE"])
                R.op("scalar", lambda e: e.activation(
                    ecum_b[:, :], cum[:, :], AF.Exp))
                R.op("scalar", lambda e: e.activation(
                    aend_v[:, :], cum[:, 127:128], AF.Exp), inc="sE")
                ec_s = cnt["sE"]
                R.wait("gpsimd", "sE", ec_s)
                R.wait("gpsimd", "tE", cnt["tE"])
                R.dma("gpsimd", dr["bounce"][3 * bs, 0:4096],
                      AP(cum, 0, [[128, 32], [1, 128]]), inc="g2")
                R.dma("gpsimd", dr["bounce_bf"][bs, 0:4096],
                      AP(ecum_b, 0, [[128, 32], [1, 128]]), inc="g2")
                gw = R.dma("gpsimd", dr["bounce"][3 * bs + 1, 0:32],
                           AP(aend_v, 0, [[1, 32], [1, 1]]), inc="g2")
                R.wait("gpsimd", "g2", gw)
                flats_g = R.dma("gpsimd", aend_c[:, :],
                                AP(dr["bounce"], (3 * bs + 1) * 4096,
                                   [[0, 32], [1, 32]]), inc="g5")

                R.wait("tensor", "sE", conv_s)
                R.wait("tensor", "vE", cnt["vE"])
                for ct in range(16):
                    R.op("tensor", lambda e, ct=ct, s0=s0: e.transpose(
                        P2[:, ct * 128:(ct + 1) * 128],
                        conv[:, ct * OC + s0:ct * OC + s0 + 128], idf[:, :]))
                R.op("tensor", lambda e, s0=s0: e.transpose(
                    P1[:, 0:32], dtv[:, s0:s0 + 128], idf[0:32, 0:32]))
                R.op("tensor", lambda e, s0=s0: e.transpose(
                    P1[:, 32:64], conv[0:32, 16 * OC + s0:16 * OC + s0 + 128],
                    idf[0:32, 0:32]), inc="tE")
                R.wait("vector", "tE", cnt["tE"])
                R.op("vector", lambda e: e.tensor_copy(xs_tm[:, :], P2[:, :]))
                R.op("vector", lambda e: e.tensor_copy(dt_tm[:, :], P1[:, 0:32]))
                R.op("vector", lambda e: e.tensor_copy(btm[:, :], P1[:, 32:64]),
                     inc="vE")
                tr_v = cnt["vE"]
                R.op("vector", lambda e: e.tensor_tensor(
                    xp[:, :], xs_tm[:, :],
                    AP(dt_tm, 0, [[32, 128], [1, 32], [0, 64]]), ALU.mult),
                    inc="vE")
                xp_v = cnt["vE"]

                cv = R.op("vector", lambda e, s0=s0: e.tensor_copy(
                    c0[:, :], conv[32:64, 16 * OC + s0:16 * OC + s0 + 128]),
                    inc="vE")
                R.wait("tensor", "vE", cv)
                R.op("tensor", lambda e, s0=s0: e.matmul(
                    P1[:, 512:640],
                    conv[0:32, 16 * OC + s0:16 * OC + s0 + 128],
                    c0[:, :], start=True, stop=True), inc="tE")
                R.wait("scalar", "tE", cnt["tE"])
                R.op("scalar", lambda e: e.copy(gt_b[:, :], P1[:, 512:640]),
                     inc="sE")
                gt_s = cnt["sE"]

                for hf in range(2):
                    R.wait("gpsimd", "tE", cnt["tE"])
                    R.dma("gpsimd", stagA[1:2, :],
                          dr["bounce"][3 * bs, hf * 2048:(hf + 1) * 2048],
                          inc="g3")
                    R.dma("gpsimd", stagB[0:1, :],
                          dr["bounce"][3 * bs, hf * 2048:(hf + 1) * 2048],
                          inc="g3")
                    half_g = R.dma("gpsimd", ecum_fl[0:1, :],
                                   dr["bounce_bf"][bs,
                                                   hf * 2048:(hf + 1) * 2048],
                                   inc="g3")
                    R.wait("tensor", "g3", half_g)
                    R.wait("tensor", "g5", flats_g)
                    R.wait("tensor", "dG", negones_g)
                    if hf == 0:
                        R.wait("tensor", "vE", tr_v)
                    for h in range(16):
                        R.op("tensor", lambda e, h=h: e.matmul(
                            P2[:, h * 128:(h + 1) * 128],
                            stagA[0:2, h * 128:(h + 1) * 128],
                            stagB[0:2, h * 128:(h + 1) * 128],
                            start=True, stop=True),
                            inc=("tE" if h == 15 else None))
                    R.wait("scalar", "tE", cnt["tE"])
                    R.op("scalar", lambda e: e.activation(dab[:, :], P2[:, :],
                                                          AF.Abs))
                    R.op("scalar", lambda e: e.activation(
                        dec[:, :], dab[:, :], AF.Exp, scale=-1.0), inc="sE")
                    dec_s = cnt["sE"]
                    R.wait("vector", "sE", dec_s)
                    R.op("vector", lambda e: e.tensor_tensor(
                        decm[:, :], dec[:, :],
                        AP(triu, 0, [[128, 128], [0, 16], [1, 128]]), ALU.mult))
                    R.wait("vector", "sE", gt_s)
                    R.op("vector", lambda e: e.tensor_tensor(
                        mt[:, :], AP(gt_b, 0, [[128, 128], [0, 16], [1, 128]]),
                        decm[:, :], ALU.mult))
                    R.op("vector", lambda e, hf=hf: e.tensor_tensor(
                        cols2[:, hf * 16:(hf + 1) * 16],
                        dt_tm[:, hf * 16:(hf + 1) * 16],
                        AP(decm, 127, [[2048, 128], [128, 16]]), ALU.mult),
                        inc="vE")
                    R.wait("vector", "vE", cnt["vE"])
                    R.wait("vector", "g4", ynb_g[0])
                    R.op("vector", lambda e, hf=hf: e.tensor_tensor(
                        xpp[:, hf * 1024:(hf + 1) * 1024],
                        xs_tm[:, hf * 1024:(hf + 1) * 1024],
                        AP(cols2, hf * 16, [[32, 128], [1, 16], [0, 64]]),
                        ALU.mult), inc="vE")
                    half_v = cnt["vE"]
                    R.wait("tensor", "sE", dec_s)
                    for q in range(4):
                        R.op("tensor", lambda e, q=q: e.matmul(
                            P2[0:32, q * 512:(q + 1) * 512],
                            C["ones_rb"][0:1, 0:32],
                            ecum_fl[0:1, q * 512:(q + 1) * 512],
                            start=True, stop=True),
                            inc=("tE" if q == 3 else None))
                    R.wait("vector", "tE", cnt["tE"])
                    R.op("vector", lambda e, s0=s0: e.tensor_tensor(
                        cpp[:, :],
                        AP(conv, 32 * HBCT * OC + 16 * OC + s0,
                           [[HBCT * OC, 32], [0, 16], [1, 128]]),
                        P2[0:32, :], ALU.mult), inc="vE")
                    cpp_v = cnt["vE"]
                    R.wait("tensor", "vE", cpp_v)
                    R.wait("tensor", "vE", xp_v)
                    for h in range(16):
                        hh = hf * 16 + h
                        ct = hh // 2
                        b0 = (hh % 2) * 64
                        R.op("tensor", lambda e, hh=hh, h=h, ct=ct, b0=b0:
                             e.matmul(
                                 P1[b0:b0 + 64, ct * 128:(ct + 1) * 128],
                                 xp[:, hh * 64:(hh + 1) * 64],
                                 mt[:, h * 128:(h + 1) * 128],
                                 start=True, stop=False))
                        R.op("tensor", lambda e, hh=hh, h=h, ct=ct, b0=b0:
                             e.matmul(
                                 P1[b0:b0 + 64, ct * 128:(ct + 1) * 128],
                                 hbf[:, hh * 64:(hh + 1) * 64],
                                 cpp[:, h * 128:(h + 1) * 128],
                                 start=False, stop=True),
                             inc=("tE" if h == 15 else None))
                y_t = cnt["tE"]

                R.wait("tensor", "vE", cnt["vE"])
                for q in range(4):
                    R.op("tensor", lambda e, q=q: e.matmul(
                        P2[0:32, q * 512:(q + 1) * 512], btm[:, :],
                        xpp[:, q * 512:(q + 1) * 512], start=True, stop=True),
                        inc=("tE" if q == 3 else None))
                st_t = cnt["tE"]

                R.wait("vector", "tE", y_t)
                R.wait("vector", "g5", flats_g)
                for ct in range(16):
                    R.op("vector", lambda e, ct=ct, s0=s0:
                         e.scalar_tensor_tensor(
                             ysub[:, ct * 128:(ct + 1) * 128],
                             conv[:, ct * OC + s0:ct * OC + s0 + 128],
                             C["d_cols"][:, ct:ct + 1],
                             P1[:, ct * 128:(ct + 1) * 128], ALU.mult, ALU.add))
                for ct in range(16):
                    R.op("vector", lambda e, ct=ct, s0=s0: e.tensor_tensor(
                        ysub[:, ct * 128:(ct + 1) * 128],
                        ysub[:, ct * 128:(ct + 1) * 128],
                        gate[:, ct * OC + s0:ct * OC + s0 + 128], ALU.mult),
                        inc=("vE" if ct == 15 else None))
                R.wait("vector", "vE", cnt["vE"])
                R.op("vector", lambda e: e.tensor_mul(
                    ysq[:, :], ysub[:, :], ysub[:, :]), inc="vE")
                yev_v = cnt["vE"]
                if P["m1dbg"] and oc == 0 and sc == 0:
                    md = dr["mdbg"]
                    R.wait("gpsimd", "vE", yev_v)
                    R.dma("gpsimd", md[:, 0:2048], ysub[:, :])
                    R.dma("gpsimd", md[0:32, 2048:2560], dts[:, :])
                    R.dma("gpsimd", md[0:32, 2560:3072], dtv[:, :])
                    R.dma("gpsimd", md[0:32, 3072:3200], cum[:, :])
                    R.dma("gpsimd", md[:, 3200:3712], lncm[:, 0:512])
                    R.dma("gpsimd", md[0:2, 3712:5760], stagA[:, :])
                    R.dma("gpsimd", md[0:32, 5760:6272], sp1[:, :])
                    R.dma("gpsimd", md[0:32, 6784:6785], C["a_col"][:, :],
                          allow_slow_non_contiguous=True)
                    R.dma("gpsimd", md[0:32, 6785:6786], C["dtb_col"][:, :],
                          allow_slow_non_contiguous=True)
                    R.dma("gpsimd", md[:, 6786:6794], stat[:, :])
                    R.wait("vector", "dG", cnt["dG"])
                    R.wait("tensor", "dG", cnt["dG"])
                    R.wait("scalar", "dG", cnt["dG"])
                R.wait("vector", "tE", st_t)
                for hh in range(32):
                    R.op("vector", lambda e, hh=hh: e.scalar_tensor_tensor(
                        hst[:, hh * 64:(hh + 1) * 64],
                        hst[:, hh * 64:(hh + 1) * 64],
                        aend_c[:, hh:hh + 1], P2[0:32, hh * 64:(hh + 1) * 64],
                        ALU.mult, ALU.add))
                R.op("vector", lambda e: e.tensor_copy(hbf[:, :], hst[:, :]),
                     inc="vE")
                hupd_v = cnt["vE"]

                R.wait("tensor", "vE", yev_v)
                for ct in range(16):
                    R.op("tensor", lambda e, ct=ct: e.matmul(
                        P1[0:1, 0:128], C["ones_cb"][:, :],
                        ysq[:, ct * 128:(ct + 1) * 128],
                        start=(ct == 0), stop=(ct == 15)),
                        inc=("tE" if ct == 15 else None))
                R.wait("scalar", "tE", cnt["tE"])
                R.op("scalar", lambda e: e.activation(
                    rstd_r[:, :], P1[0:1, 0:128], AF.Ln, scale=1.0 / D,
                    bias=C["eps_rms"][:, :]))
                R.op("scalar", lambda e: e.activation(
                    rstd_r[:, :], rstd_r[:, :], AF.Exp, scale=-0.5), inc="sE")
                R.wait("gpsimd", "sE", cnt["sE"])
                g2w = R.dma("gpsimd", dr["bounce"][3 * bs + 2, 0:128],
                            rstd_r[0:1, :], inc="g2")
                R.wait("gpsimd", "g2", g2w)
                gw2 = R.dma("gpsimd", rstd_bc[:, :],
                            AP(dr["bounce"], (3 * bs + 2) * 4096,
                               [[0, 128], [1, 128]]), inc="g3")
                R.wait("vector", "g3", gw2)
                R.op("vector", lambda e: e.tensor_tensor(
                    ynb[:, :], ysub[:, :],
                    AP(rstd_bc, 0, [[128, 128], [0, 16], [1, 128]]), ALU.mult),
                    inc="vE")
                ynb_v = cnt["vE"]
                R.wait("gpsimd", "vE", ynb_v)
                for g4 in range(4):
                    dsty = AP(dr["ynormb"], (g4 * 4 * 128) * L + gtok,
                              [[L, 128], [128 * L, 4], [1, 128]])
                    srcy = AP(ynb, g4 * 4 * 128, [[2048, 128], [128, 4], [1, 128]])
                    ynb_g[0] = R.dma("gpsimd", dsty, srcy, inc="g4")
                R.wait("tensor", "vE", hupd_v)
                R.wait("tensor", "sE", cnt["sE"])
        R.fence()
        R.emit()



def phase_m2(P):
    nc, sems, cnt, di, dr = P["nc"], P["sems"], P["cnt"], P["di"], P["dr"]
    R = Rec(nc, sems, cnt)
    with ExitStack() as st:
        wsb = st.enter_context(nc.sbuf_tensor("m2_w", [128, 16 * 2048], BF16))
        yt = [st.enter_context(nc.sbuf_tensor("m2_y%d" % i, [128, 2048], BF16))
              for i in range(2)]
        xt = [st.enter_context(nc.sbuf_tensor("m2_x%d" % i, [128, 2048], F32))
              for i in range(2)]
        ot = [st.enter_context(nc.sbuf_tensor("m2_o%d" % i, [128, 2048], F32))
              for i in range(2)]
        PSq = [st.enter_context(nc.psum_tensor("m2_ps%d" % i, [128, 2048], F32))
               for i in range(2)]

        for ct in range(16):
            R.dma("sync", wsb[:, ct * 2048:(ct + 1) * 2048],
                  dr["woutb"][ct * 128:(ct + 1) * 128, :])
        w_d = cnt["dW"]
        yfree = [0, 0]
        ofree = [0, 0]
        xfree = [0, 0]
        psfree = [0, 0]
        for t in range(32):
            p = t % 2
            R.wait("gpsimd", "tE", yfree[p])
            gy = R.dma("gpsimd",
                       AP(yt[p], 0, [[2048, 128], [128, 16], [1, 128]]),
                       AP(dr["ynormb"], t * 128,
                          [[L, 128], [128 * L, 16], [1, 128]]), inc="g%d" % p)
            R.wait("gpsimd", "vE", xfree[p])
            gx = R.dma("gpsimd", xt[p][:, :],
                       di["x"][t * 128:(t + 1) * 128, :], inc="g%d" % (2 + p))
            R.wait("tensor", "dW", w_d)
            R.wait("tensor", "g%d" % p, gy)
            R.wait("tensor", "vE", psfree[p])
            for dg in range(4):
                for ct in range(16):
                    R.op("tensor", lambda e, p=p, dg=dg, ct=ct: e.matmul(
                        PSq[p][:, dg * 512:(dg + 1) * 512],
                        yt[p][:, ct * 128:(ct + 1) * 128],
                        wsb[:, ct * 2048 + dg * 512:ct * 2048 + (dg + 1) * 512],
                        start=(ct == 0), stop=(ct == 15)),
                        inc=("tE" if (dg == 3 and ct == 15) else None))
            yfree[p] = cnt["tE"]
            R.wait("vector", "tE", cnt["tE"])
            R.wait("vector", "g%d" % (2 + p), gx)
            R.wait("vector", "g%d" % (4 + p), ofree[p])
            psfree[p] = xfree[p] = R.op(
                "vector", lambda e, p=p: e.scalar_tensor_tensor(
                    ot[p][:, :], xt[p][:, :], 1.0, PSq[p][:, :],
                    ALU.mult, ALU.add), inc="vE")
            R.wait("gpsimd", "vE", cnt["vE"])
            ofree[p] = R.dma("gpsimd", dr["xbuf"][t * 128:(t + 1) * 128, :],
                             ot[p][:, :], inc="g%d" % (4 + p))
        R.fence()
        R.emit()



_FFN_N = [0]


def phase_ffn(P, src, dst):
    nc, sems, cnt, dr, C = P["nc"], P["sems"], P["cnt"], P["dr"], P["C"]
    R = Rec(nc, sems, cnt)
    tag = "f%d" % _FFN_N[0]
    _FFN_N[0] += 1
    with ExitStack() as st:
        def sba(name, shape, dt=F32):
            return st.enter_context(nc.sbuf_tensor(tag + name, list(shape), dt))
        xt = [sba("_x%d" % i, (128, 2048)) for i in range(2)]
        lnt = sba("_lnt", (128, 2048))
        lnsq = sba("_lnsq", (128, 2048))
        lncm = sba("_lncm", (128, 16 * FT), BF16)
        hsp = sba("_hsp", (128, 64 * FT), BF16)
        wgt = [sba("_wgt%d" % i, (128, 2048), BF16) for i in range(2)]
        wut = [sba("_wut%d" % i, (128, 2048), BF16) for i in range(2)]
        gsil = sba("_gsil", (128, 512))
        hq = [sba("_hq%d" % i, (128, 512)) for i in range(4)]
        aq = [sba("_aq%d" % i, (128, 512)) for i in range(4)]
        tau = sba("_tau", (128, 512))
        msk = sba("_msk", (128, 512))
        t5 = sba("_t5", (128, 512))
        t6 = sba("_t6", (128, 512))
        wdt = [sba("_wd%d" % i, (128, 2048), BF16) for i in range(2)]
        xs2 = [sba("_xs%d" % i, (128, 512)) for i in range(2)]
        os2 = [sba("_os%d" % i, (128, 512)) for i in range(2)]
        stat = sba("_stat", (128, 8))
        PH = st.enter_context(nc.psum_tensor(tag + "_ph", [128, 2048], F32))
        PD = st.enter_context(nc.psum_tensor(tag + "_pd", [128, 2048], F32))

        xfree = [0, 0]
        xs2free = [0, 0]
        os2free = [0, 0]
        wfree = [0, 0]
        wdfree = [0, 0]
        phfree = [0, 0]
        hq_done = []
        for fp in range(NFP):
            tok0 = fp * FT
            R.wait("tensor", "vE", cnt["vE"])
            for tt in range(4):
                p = tt % 2
                R.wait("gpsimd", "vE", xfree[p])
                gl = R.dma("gpsimd", xt[p][:, :],
                           src[tok0 + tt * 128:tok0 + (tt + 1) * 128, :],
                           inc="g%d" % p)
                xfree[p] = emit_ln(R, P, xt[p], lnt, lnsq, stat, gl,
                                   dsem="g%d" % p)
                emit_ln_transpose(R, P, lnt, lncm, PD, tt, FT)
            R.wait("tensor", "sE", cnt["sE"])
            ln_s = cnt["sE"]

            for ic in range(16):
                for j in range(4):
                    it = 4 * ic + j
                    p = it % 2
                    ph0 = (it % 2) * 1024
                    R.wait("sync", "tE", wfree[p])
                    wg_pos = R.dma("sync", AP(wgt[p], 0,
                                              [[2048, 128], [128, 16], [1, 128]]),
                                   dr["wgb"][it * 2048:(it + 1) * 2048, :],
                                   inc="g%d" % (6 + p))
                    wu_pos = R.dma("sync", AP(wut[p], 0,
                                              [[2048, 128], [128, 16], [1, 128]]),
                                   dr["wub"][it * 2048:(it + 1) * 2048, :],
                                   inc="g%d" % (8 + p))
                    R.wait("tensor", "g%d" % (6 + p), wg_pos)
                    R.wait("tensor", "sE", ln_s)
                    R.wait("tensor", "vE", phfree[it % 2])
                    for ct in range(16):
                        R.op("tensor", lambda e, p=p, ct=ct, ph0=ph0: e.matmul(
                            PH[:, ph0:ph0 + 512],
                            wgt[p][:, ct * 128:(ct + 1) * 128],
                            lncm[:, ct * FT:ct * FT + 512],
                            start=(ct == 0), stop=(ct == 15)))
                    R.wait("tensor", "g%d" % (8 + p), wu_pos)
                    for ct in range(16):
                        R.op("tensor", lambda e, p=p, ct=ct, ph0=ph0: e.matmul(
                            PH[:, ph0 + 512:ph0 + 1024],
                            wut[p][:, ct * 128:(ct + 1) * 128],
                            lncm[:, ct * FT:ct * FT + 512],
                            start=(ct == 0), stop=(ct == 15)),
                            inc=("tE" if ct == 15 else None))
                    wfree[p] = cnt["tE"]
                    R.wait("scalar", "tE", cnt["tE"])
                    if hq_done:
                        R.wait("scalar", "vE", hq_done[-1])
                    R.op("scalar", lambda e, ph0=ph0: e.activation(
                        gsil[:, :], PH[:, ph0:ph0 + 512], AF.Silu), inc="sE")
                    R.wait("vector", "sE", cnt["sE"])
                    hq_done.append(R.op(
                        "vector", lambda e, j=j, ph0=ph0: e.tensor_tensor(
                            hq[j][:, :], gsil[:, :], PH[:, ph0 + 512:ph0 + 1024],
                            ALU.mult), inc="vE"))
                    phfree[it % 2] = cnt["vE"]
                    R.wait("scalar", "vE", cnt["vE"])
                    R.op("scalar", lambda e, j=j: e.activation(
                        aq[j][:, :], hq[j][:, :], AF.Abs), inc="sE")
                abs_s = cnt["sE"]
                R.wait("vector", "sE", abs_s)
                R.op("vector", lambda e: e.tensor_max(tau[:, :], aq[0][:, :],
                                                      aq[1][:, :]))
                R.op("vector", lambda e: e.tensor_tensor(
                    msk[:, :], aq[0][:, :], aq[1][:, :], ALU.min))
                R.op("vector", lambda e: e.tensor_max(t5[:, :], aq[2][:, :],
                                                      aq[3][:, :]))
                R.op("vector", lambda e: e.tensor_tensor(
                    t6[:, :], aq[2][:, :], aq[3][:, :], ALU.min))
                R.op("vector", lambda e: e.tensor_tensor(
                    tau[:, :], tau[:, :], t5[:, :], ALU.min))
                R.op("vector", lambda e: e.tensor_max(msk[:, :], msk[:, :],
                                                      t6[:, :]), inc="vE")
                R.wait("vector", "vE", cnt["vE"])
                R.op("vector", lambda e: e.tensor_max(tau[:, :], tau[:, :],
                                                      msk[:, :]), inc="vE")
                R.wait("vector", "vE", cnt["vE"])
                for j in range(4):
                    it = 4 * ic + j
                    R.op("vector", lambda e, j=j: e.tensor_tensor(
                        msk[:, :], aq[j][:, :], tau[:, :], ALU.is_ge))
                    R.op("vector", lambda e, j=j, it=it: e.tensor_tensor(
                        hsp[:, it * FT:(it + 1) * FT], hq[j][:, :],
                        msk[:, :], ALU.mult),
                        inc=("vE" if j == 3 else None))
            hsp_v = cnt["vE"]

            for cg in range(4):
                if cg > 0:
                    R.wait("tensor", "vE", cnt["vE"])
                for gg in range(16):
                    p = gg % 2
                    R.wait("sync", "tE", wdfree[p])
                    wd_pos = R.dma(
                        "sync", AP(wdt[p], 0, [[2048, 128], [512, 4], [1, 512]]),
                        AP(dr["wdb"], 512 * gg * D + cg * 512,
                           [[4 * D, 128], [D, 4], [1, 512]]),
                        inc="g%d" % (2 + p))
                    R.wait("tensor", "g%d" % (2 + p), wd_pos)
                    if gg == 0:
                        R.wait("tensor", "vE", hsp_v)
                    for j4 in range(4):
                        kt = 4 * gg + j4
                        for tt in range(4):
                            R.op("tensor",
                                 lambda e, p=p, kt=kt, tt=tt, j4=j4: e.matmul(
                                     PD[:, tt * 512:(tt + 1) * 512],
                                     hsp[:, kt * FT + tt * 128:
                                         kt * FT + (tt + 1) * 128],
                                     wdt[p][:, j4 * 512:(j4 + 1) * 512],
                                     start=(kt == 0), stop=(kt == 63)),
                                 inc=("tE" if (j4 == 3 and tt == 3) else None))
                    wdfree[p] = cnt["tE"]
                R.wait("vector", "tE", cnt["tE"])
                for tt in range(4):
                    pp = tt % 2
                    R.wait("sync", "vE", xs2free[pp])
                    gx = R.dma("sync", xs2[pp][:, :],
                               AP(src, (tok0 + tt * 128) * D + cg * 512,
                                  [[D, 128], [1, 512]]),
                               inc=("gA" if pp == 0 else "gB"))
                    R.wait("vector", "gA" if pp == 0 else "gB", gx)
                    R.wait("vector", "g%d" % (4 + pp), os2free[pp])
                    xs2free[pp] = R.op(
                        "vector", lambda e, pp=pp, tt=tt:
                        e.scalar_tensor_tensor(
                            os2[pp][:, :], xs2[pp][:, :], 1.0,
                            PD[:, tt * 512:(tt + 1) * 512],
                            ALU.mult, ALU.add), inc="vE")
                    R.wait("gpsimd", "vE", cnt["vE"])
                    os2free[pp] = R.dma(
                        "gpsimd", AP(dst, (tok0 + tt * 128) * D + cg * 512,
                                     [[D, 128], [1, 512]]), os2[pp][:, :],
                        inc="g%d" % (4 + pp))
        R.fence()
        R.emit()



def phase_a1(P):
    nc, sems, cnt, dr, C = P["nc"], P["sems"], P["cnt"], P["dr"], P["C"]
    R = Rec(nc, sems, cnt)
    with ExitStack() as st:
        def sba(name, shape, dt=F32):
            return st.enter_context(nc.sbuf_tensor(name, list(shape), dt))
        xt = [sba("a1_x%d" % i, (128, 2048)) for i in range(2)]
        lnt = sba("a1_lnt", (128, 2048))
        lnsq = sba("a1_lnsq", (128, 2048))
        lncm = sba("a1_lncm", (128, 16 * 2048), BF16)
        wq4 = [sba("a1_w%d" % i, (128, 16 * 512), BF16) for i in range(2)]
        ev = [sba("a1_e%d" % i, (128, 512), BF16) for i in range(2)]
        stat = sba("a1_stat", (128, 8))
        PS = st.enter_context(nc.psum_tensor("a1_ps", [128, 1024], F32))

        xfree = [0, 0]
        wfree = [0, 0]
        ev_sfree = [0, 0]
        ev_gfree = [0, 0]
        evfree2 = [0, 0]
        for hs in range(2):
            t0 = hs * 2048
            R.wait("tensor", "vE", cnt["vE"])
            for tt in range(16):
                p = tt % 2
                R.wait("gpsimd", "vE", xfree[p])
                gl = R.dma("gpsimd", xt[p][:, :],
                           dr["xbuf"][t0 + tt * 128:t0 + (tt + 1) * 128, :],
                           inc="g%d" % p)
                xfree[p] = emit_ln(R, P, xt[p], lnt, lnsq, stat, gl,
                                   dsem="g%d" % p)
                emit_ln_transpose(R, P, lnt, lncm, PS, tt, 2048, nslots=2)
            R.wait("tensor", "sE", cnt["sE"])
            ln_s = cnt["sE"]

            for (wd, dst_d, bc) in ((dr["wqTb"], dr["qb_d"], C["bq_c"]),
                                    (dr["wkTb"], dr["kb_d"], C["bk_c"])):
                for q in range(4):
                    p = q % 2
                    R.wait("sync", "tE", wfree[p])
                    wpos = R.dma("sync",
                                 AP(wq4[p], 0, [[8192, 128], [512, 16], [1, 512]]),
                                 AP(wd, q * 512, [[D, 128], [128 * D, 16],
                                                  [1, 512]]),
                                 inc="g%d" % (6 + p))
                    for dt4 in range(4):
                        dpt = q * 4 + dt4
                        for tc in range(4):
                            pe = (dt4 * 4 + tc) % 2
                            R.wait("tensor", "g%d" % (6 + p), wpos)
                            R.wait("tensor", "sE", ln_s)
                            R.wait("tensor", "sE", ev_sfree[pe])
                            R.wait("tensor", "g%d" % (2 + pe), evfree2[pe])
                            for ct in range(16):
                                R.op("tensor", lambda e, p=p, ct=ct, dt4=dt4,
                                     tc=tc, pe=pe: e.matmul(
                                         PS[:, pe * 512:(pe + 1) * 512],
                                         wq4[p][:, ct * 512 + dt4 * 128:
                                                ct * 512 + (dt4 + 1) * 128],
                                         lncm[:, ct * 2048 + tc * 512:
                                              ct * 2048 + (tc + 1) * 512],
                                         start=(ct == 0), stop=(ct == 15)),
                                     inc=("tE" if ct == 15 else None))
                            R.wait("scalar", "tE", cnt["tE"])
                            R.wait("scalar", "g%d" % (4 + pe), ev_gfree[pe])
                            ev_sfree[pe] = R.op(
                                "scalar", lambda e, pe=pe, dpt=dpt, bc=bc:
                                e.activation(
                                    ev[pe][:, :], PS[:, pe * 512:(pe + 1) * 512],
                                    AF.Identity, bias=bc[:, dpt:dpt + 1]), inc="sE")
                            R.wait("gpsimd", "sE", cnt["sE"])
                            ev_gfree[pe] = R.dma(
                                "gpsimd",
                                AP(dst_d, dpt * 128 * L + t0 + tc * 512,
                                   [[L, 128], [1, 512]]), ev[pe][:, :],
                                inc="g%d" % (4 + pe))
                    wfree[p] = cnt["tE"]

            for q in range(4):
                p = q % 2
                R.wait("sync", "tE", wfree[p])
                wpos = R.dma("sync",
                             AP(wq4[p], 0, [[8192, 128], [512, 16], [1, 512]]),
                             AP(dr["wvTb"], q * 512,
                                [[D, 128], [128 * D, 16], [1, 512]]),
                             inc="g%d" % (6 + p))
                for tt in range(16):
                    pe = tt % 2
                    R.wait("tensor", "g%d" % (6 + p), wpos)
                    R.wait("tensor", "sE", ln_s)
                    R.wait("tensor", "g%d" % (2 + pe), evfree2[pe])
                    R.wait("tensor", "sE", ev_sfree[pe])
                    for ct in range(16):
                        R.op("tensor", lambda e, p=p, ct=ct, tt=tt, pe=pe:
                             e.matmul(
                                 PS[:, pe * 512:(pe + 1) * 512],
                                 lncm[:, ct * 2048 + tt * 128:
                                      ct * 2048 + (tt + 1) * 128],
                                 wq4[p][:, ct * 512:(ct + 1) * 512],
                                 start=(ct == 0), stop=False))
                    R.op("tensor", lambda e, pe=pe, q=q: e.matmul(
                        PS[:, pe * 512:(pe + 1) * 512],
                        C["ones_rb"][0:1, 0:128],
                        C["bv_r"][0:1, q * 512:(q + 1) * 512],
                        start=False, stop=True), inc="tE")
                    R.wait("vector", "tE", cnt["tE"])
                    R.wait("vector", "g%d" % (2 + pe), evfree2[pe])
                    evfree2[pe] = R.op(
                        "vector", lambda e, pe=pe: e.tensor_copy(
                            ev[pe][:, :], PS[:, pe * 512:(pe + 1) * 512]),
                        inc="vE")
                    R.wait("gpsimd", "vE", cnt["vE"])
                    for hh4 in range(4):
                        h = q * 4 + hh4
                        evfree2[pe] = R.dma(
                            "gpsimd",
                            dr["vb_d"][h * L + t0 + tt * 128:
                                       h * L + t0 + (tt + 1) * 128, :],
                            ev[pe][:, hh4 * 128:(hh4 + 1) * 128],
                            inc="g%d" % (2 + pe))
                wfree[p] = cnt["tE"]
        R.fence()
        R.emit()


def phase_a2(P):
    nc, sems, cnt, dr, C = P["nc"], P["sems"], P["cnt"], P["dr"], P["C"]
    R = Rec(nc, sems, cnt)
    with ExitStack() as st:
        def sba(name, shape, dt=F32):
            return st.enter_context(nc.sbuf_tensor(name, list(shape), dt))
        qh = [sba("a2_q%d" % i, (128, L), BF16) for i in range(2)]
        kh = [sba("a2_k%d" % i, (128, L), BF16) for i in range(2)]
        vh = [sba("a2_v%d" % i, (128, 32 * 128), BF16) for i in range(2)]
        p4s = [sba("a2_p4s%d" % i, (128, 32 * 512), BF16) for i in range(2)]
        pt = [sba("a2_pt%d" % i, (128, 512), BF16) for i in range(4)]
        sq = [sba("a2_sq%d" % i, (128, 512), BF16) for i in range(4)]
        zb = sba("a2_zb", (128, 512), BF16)
        mrow = sba("a2_mrow", (1, 512))
        z_b = sba("a2_zrow", (1, 512), BF16)
        oa = [sba("a2_oa%d" % i, (128, 512), BF16) for i in range(2)]
        PSA = st.enter_context(nc.psum_tensor("a2_psa", [128, 2048], F32))
        PO = st.enter_context(nc.psum_tensor("a2_po", [128, 1024], F32))
        PCS = st.enter_context(nc.psum_tensor("a2_pcs", [1, 512], F32))

        hfree = [0, 0]
        po_free = [0, 0]
        pcs_free = [0]
        psa_free = [0, 0, 0, 0]
        s4_guard = [0]
        s4_dma = [0]
        zrow_guard = [0]
        oa_store = [0, 0]
        sqfree = [0, 0, 0, 0]
        sq_done = {}
        ptfree = [0, 0, 0, 0]
        p4read = [[0] * 32, [0] * 32]
        gv_p = [0, 0]
        LA = 3
        jobs = [(h, 7 - q) for h in range(NH) for q in range(8)]
        exp_sd = {}
        cs_last = {}
        ptv = {}
        pvt = {}
        s4s = {}

        def emit_loads(h):
            p = h % 2
            R.wait("gpsimd", "tE", hfree[p])
            R.dma("gpsimd", qh[p][:, :], dr["qb_d"][h * 128:(h + 1) * 128, :],
                  inc="g%d" % p)
            R.dma("gpsimd", kh[p][:, :], dr["kb_d"][h * 128:(h + 1) * 128, :],
                  inc="g%d" % p)
            for q4 in range(4):
                gv_p[p] = R.dma(
                    "gpsimd",
                    AP(vh[p], q4 * 8 * 128, [[4096, 128], [128, 8], [1, 128]]),
                    AP(dr["vb_d"], (h * L + q4 * 1024) * DH,
                       [[DH, 128], [128 * DH, 8], [1, DH]]), inc="g%d" % p)

        def emit_pass1_pair(i, kt):
            h, qb = jobs[i]
            p = h % 2
            pf = i % 2
            pe = kt % 4
            if kt == 0:
                R.wait("tensor", "g%d" % p, gv_p[p])
            R.wait("tensor", "sE", max(psa_free[pe], psa_free[pe + 1]))
            R.op("tensor", lambda e, p=p, kt=kt, qb=qb, pe=pe:
                 e.matmul(PSA[:, pe * 512:(pe + 1) * 512],
                          kh[p][:, kt * 128:(kt + 1) * 128],
                          qh[p][:, qb * 512:(qb + 1) * 512],
                          start=True, stop=True))
            mm_t = R.op("tensor", lambda e, p=p, kt=kt + 1, qb=qb, pe=pe + 1:
                        e.matmul(PSA[:, pe * 512:(pe + 1) * 512],
                                 kh[p][:, kt * 128:(kt + 1) * 128],
                                 qh[p][:, qb * 512:(qb + 1) * 512],
                                 start=True, stop=True), inc="tE")
            R.wait("scalar", "tE", mm_t)
            R.wait("scalar", "vE", max(p4read[pf][kt], p4read[pf][kt + 1]))
            R.op("scalar", lambda e, pe=pe, kt=kt, pf=pf: e.activation(
                p4s[pf][:, kt * 512:(kt + 1) * 512],
                PSA[:, pe * 512:(pe + 1) * 512],
                AF.Exp, scale=0.5))
            s = R.op("scalar", lambda e, pe=pe + 1, kt=kt + 1, pf=pf:
                     e.activation(p4s[pf][:, kt * 512:(kt + 1) * 512],
                                  PSA[:, pe * 512:(pe + 1) * 512],
                                  AF.Exp, scale=0.5), inc="sE")
            exp_sd[i][kt] = s
            exp_sd[i][kt + 1] = s
            psa_free[pe] = s
            psa_free[pe + 1] = s
            for k2 in (kt, kt + 1):
                sl = k2 % 4
                r = k2 - 4 * qb
                if r >= 0:
                    R.wait("vector", "sE", s)
                    R.wait("vector", "tE", sqfree[sl])
                    if r > 0:
                        R.op("vector", lambda e, pf=pf, kt=k2, r=r: e.memset(
                            p4s[pf][:, kt * 512:kt * 512 + r * 128], 0.0))
                    R.op("vector", lambda e, pf=pf, kt=k2, r=r: e.tensor_tensor(
                        p4s[pf][:, kt * 512 + r * 128:kt * 512 + (r + 1) * 128],
                        p4s[pf][:, kt * 512 + r * 128:kt * 512 + (r + 1) * 128],
                        C["triu_b"][:, :], ALU.mult))
                    c = R.op("vector", lambda e, pf=pf, kt=k2, sl=sl:
                             e.tensor_tensor(
                                 sq[sl][:, :],
                                 p4s[pf][:, kt * 512:(kt + 1) * 512],
                                 p4s[pf][:, kt * 512:(kt + 1) * 512],
                                 ALU.mult), inc="vE")
                    sq_done[k2] = ("vE", c)
                elif k2 % 2 == 0:
                    R.wait("scalar", "tE", sqfree[sl])
                    c = R.op("scalar", lambda e, pf=pf, kt=k2, sl=sl:
                             e.activation(sq[sl][:, :],
                                          p4s[pf][:, kt * 512:(kt + 1) * 512],
                                          AF.Square), inc="sE")
                    sq_done[k2] = ("sE", c)
                else:
                    R.wait("vector", "sE", s)
                    R.wait("vector", "tE", sqfree[sl])
                    c = R.op("vector", lambda e, pf=pf, kt=k2, sl=sl:
                             e.tensor_tensor(
                                 sq[sl][:, :],
                                 p4s[pf][:, kt * 512:(kt + 1) * 512],
                                 p4s[pf][:, kt * 512:(kt + 1) * 512],
                                 ALU.mult), inc="vE")
                    sq_done[k2] = ("vE", c)

        def emit_cs_pair(i, j):
            h, qb = jobs[i]
            kt_hi = 4 * qb + 4
            R.wait("tensor", sq_done[j][0], sq_done[j][1])
            R.wait("tensor", sq_done[j + 1][0], sq_done[j + 1][1])
            if j == 0:
                R.wait("tensor", "sE", pcs_free[0])
            R.op("tensor", lambda e, sl=j % 4:
                 e.matmul(PCS[:, :], C["ones_cb"][:, :], sq[sl][:, :],
                          start=(j == 0), stop=False))
            c = R.op("tensor", lambda e, sl=(j + 1) % 4, last=(j + 1 == kt_hi - 1):
                     e.matmul(PCS[:, :], C["ones_cb"][:, :], sq[sl][:, :],
                              start=False, stop=last), inc="tE")
            sqfree[j % 4] = c
            sqfree[(j + 1) % 4] = c
            if j + 1 == kt_hi - 1:
                cs_last[i] = c

        def emit_between(i):
            R.wait("scalar", "tE", cs_last[i])
            pcs_free[0] = R.op("scalar", lambda e: e.activation(
                mrow[:, :], PCS[:, :], AF.Ln), inc="sE")
            R.wait("scalar", "gD", zrow_guard[0])
            zs = R.op("scalar", lambda e: e.activation(
                z_b[:, :], mrow[:, :], AF.Exp, scale=-0.5), inc="sE")
            R.wait("gpsimd", "sE", zs)
            R.wait("gpsimd", "vE", s4_guard[0])
            d1 = R.dma("gpsimd", dr["bounce_bf"][12, 0:512], z_b[0:1, :],
                       inc="gD")
            zrow_guard[0] = d1
            R.wait("gpsimd", "gD", d1)
            s4_dma[0] = R.dma("gpsimd", zb[:, :],
                              AP(dr["bounce_bf"], 12 * 4096,
                                 [[0, 128], [1, 512]]), inc="gD")
            s4s[i] = s4_dma[0]

        def emit_pass2_vec_pair(i, kt):
            h, qb = jobs[i]
            kt_hi = 4 * qb + 4
            pf = i % 2
            pe = kt % 4
            if kt == 0:
                R.wait("vector", "gD", s4s[i])
            R.wait("vector", "tE", max(ptfree[pe], ptfree[pe + 1]))
            v = 0
            for k2 in (kt, kt + 1):
                pe2 = k2 % 4
                R.op("vector", lambda e, pe=pe2, kt=k2, pf=pf: e.tensor_tensor(
                    pt[pe][:, :], p4s[pf][:, kt * 512:(kt + 1) * 512],
                    zb[:, :], ALU.mult))
                v = R.op("vector", lambda e, pe=pe2: e.tensor_tensor(
                    pt[pe][:, :], pt[pe][:, :], pt[pe][:, :], ALU.mult),
                    inc=("vE" if k2 == kt + 1 else None))
            ptv[i][kt] = v
            ptv[i][kt + 1] = v
            p4read[pf][kt] = v
            p4read[pf][kt + 1] = v
            if kt + 1 == kt_hi - 1:
                s4_guard[0] = v

        def emit_pass2_mm_pair(i, j):
            h, qb = jobs[i]
            kt_hi = 4 * qb + 4
            p = h % 2
            op = i % 2
            R.wait("tensor", "vE", ptv[i][j + 1])
            if j == 0:
                R.wait("tensor", "sE", po_free[op])
            t = 0
            for j2 in (j, j + 1):
                pe2 = j2 % 4
                t = R.op("tensor", lambda e, p=p, j=j2, pe=pe2, op=op,
                         kt_hi=kt_hi: e.matmul(
                             PO[:, op * 512:(op + 1) * 512],
                             vh[p][:, j * 128:(j + 1) * 128],
                             pt[pe][:, :], start=(j == 0),
                             stop=(j == kt_hi - 1)),
                         inc=("tE" if j2 == j + 1 else None))
            ptfree[j % 4] = t
            ptfree[(j + 1) % 4] = t
            pvt[i][j] = t
            pvt[i][j + 1] = t

        def emit_epilogue(i):
            h, qb = jobs[i]
            kt_hi = 4 * qb + 4
            op = i % 2
            R.wait("scalar", "tE", pvt[i][kt_hi - 1])
            R.wait("scalar", "g4", oa_store[op])
            c = R.op("scalar", lambda e, op=op: e.activation(
                oa[op][:, :], PO[:, op * 512:(op + 1) * 512], AF.Identity),
                inc="sE")
            po_free[op] = c
            R.wait("gpsimd", "sE", c)
            oa_store[op] = R.dma(
                "gpsimd", AP(dr["attnb"], h * 128 * L + qb * 512,
                             [[L, 128], [1, 512]]), oa[op][:, :], inc="g4")
            if qb == 0:
                hfree[h % 2] = cnt["tE"]

        LAP = 2
        emit_loads(0)
        exp_sd[0] = {}
        n0 = (4 * jobs[0][1] + 4) // 2
        for step in range(n0 + LAP):
            if 0 <= step - LAP < n0:
                emit_cs_pair(0, 2 * (step - LAP))
            if step < n0:
                emit_pass1_pair(0, 2 * step)
        for i in range(len(jobs)):
            h, qb = jobs[i]
            if qb == 4 and h + 1 < NH:
                emit_loads(h + 1)
            emit_between(i)
            n2 = (4 * qb + 4) // 2
            i1 = i + 1 if i + 1 < len(jobs) else None
            n1 = ((4 * jobs[i1][1] + 4) // 2) if i1 is not None else 0
            if i1 is not None:
                exp_sd[i1] = {}
            ptv[i] = {}
            pvt[i] = {}
            for step in range(max(n1, n2) + LAP):
                if i1 is not None and 0 <= step - LAP < n1:
                    emit_cs_pair(i1, 2 * (step - LAP))
                if i1 is not None and step < n1:
                    emit_pass1_pair(i1, 2 * step)
                if 0 <= step - LAP < n2:
                    emit_pass2_mm_pair(i, 2 * (step - LAP))
                if step < n2:
                    emit_pass2_vec_pair(i, 2 * step)
            emit_epilogue(i)
        R.fence()
        R.emit()


def phase_a3(P):
    nc, sems, cnt, dr, C = P["nc"], P["sems"], P["cnt"], P["dr"], P["C"]
    R = Rec(nc, sems, cnt)
    with ExitStack() as st:
        wsb = st.enter_context(nc.sbuf_tensor("a3_w", [128, 16 * 2048], BF16))
        at = [st.enter_context(nc.sbuf_tensor("a3_a%d" % i, [128, 2048], BF16))
              for i in range(2)]
        xt = [st.enter_context(nc.sbuf_tensor("a3_x%d" % i, [128, 2048], F32))
              for i in range(2)]
        ot = [st.enter_context(nc.sbuf_tensor("a3_o%d" % i, [128, 2048], F32))
              for i in range(2)]
        PSq = [st.enter_context(nc.psum_tensor("a3_ps%d" % i, [128, 2048], F32))
               for i in range(2)]

        for ct in range(16):
            R.dma("sync", wsb[:, ct * 2048:(ct + 1) * 2048],
                  dr["woTb"][ct * 128:(ct + 1) * 128, :])
        w_d = cnt["dW"]
        afree = [0, 0]
        xfree = [0, 0]
        ofree = [0, 0]
        psfree = [0, 0]
        for t in range(32):
            p = t % 2
            R.wait("gpsimd", "tE", afree[p])
            ga = R.dma("gpsimd",
                       AP(at[p], 0, [[2048, 128], [128, 16], [1, 128]]),
                       AP(dr["attnb"], t * 128,
                          [[L, 128], [128 * L, 16], [1, 128]]), inc="g%d" % p)
            R.wait("gpsimd", "vE", xfree[p])
            gx = R.dma("gpsimd", xt[p][:, :],
                       dr["xbuf"][t * 128:(t + 1) * 128, :], inc="g%d" % (2 + p))
            R.wait("tensor", "dW", w_d)
            R.wait("tensor", "g%d" % p, ga)
            R.wait("tensor", "vE", psfree[p])
            for dg in range(4):
                for ct in range(16):
                    R.op("tensor", lambda e, p=p, dg=dg, ct=ct: e.matmul(
                        PSq[p][:, dg * 512:(dg + 1) * 512],
                        at[p][:, ct * 128:(ct + 1) * 128],
                        wsb[:, ct * 2048 + dg * 512:ct * 2048 + (dg + 1) * 512],
                        start=(ct == 0), stop=False))
                R.op("tensor", lambda e, p=p, dg=dg: e.matmul(
                    PSq[p][:, dg * 512:(dg + 1) * 512], C["ones_rb"][0:1, 0:128],
                    C["bo_r"][0:1, dg * 512:(dg + 1) * 512],
                    start=False, stop=True),
                    inc=("tE" if dg == 3 else None))
            afree[p] = cnt["tE"]
            R.wait("vector", "tE", cnt["tE"])
            R.wait("vector", "g%d" % (2 + p), gx)
            R.wait("vector", "g%d" % (4 + p), ofree[p])
            psfree[p] = xfree[p] = R.op(
                "vector", lambda e, p=p: e.scalar_tensor_tensor(
                    ot[p][:, :], xt[p][:, :], 1.0, PSq[p][:, :],
                    ALU.mult, ALU.add), inc="vE")
            R.wait("gpsimd", "vE", cnt["vE"])
            ofree[p] = R.dma("gpsimd", dr["xbuf"][t * 128:(t + 1) * 128, :],
                             ot[p][:, :], inc="g%d" % (4 + p))
        R.fence()
        R.emit()



def make_inputs(full):
    x = np.ascontiguousarray(full["L0_attn_a_x"])
    W_in = np.zeros((D, PROJ), np.float32)
    W_in[:, :4192] = full["L0_attn_a_W_in"]
    b_in = np.zeros((PROJ,), np.float32)
    b_in[:4192] = full["L0_attn_a_b_in"]
    conv_w = np.zeros((HBCT * 128, 4), np.float32)
    conv_w[:2112] = full["L0_attn_a_conv_w"]
    conv_b = np.zeros((HBCT * 128,), np.float32)
    conv_b[:2112] = full["L0_attn_a_conv_b"]

    com = {
        "W_in": W_in,
        "b_in_cols": b_in.reshape(NPO, 128).T,
        "conv_w_cols": conv_w.reshape(HBCT, 128, 4).transpose(1, 0, 2
                                                              ).reshape(128, HBCT * 4),
        "conv_b_cols": conv_b.reshape(HBCT, 128).T,
        "A_col": full["L0_attn_a_A_log"].reshape(32, 1),
        "dtb_col": full["L0_attn_a_dt_bias"].reshape(32, 1),
        "D_cols": full["L0_attn_a_D_param"].reshape(16, 2)[:, :, None].repeat(
            64, axis=2).reshape(16, 128).T,
        "norm_cols": full["L0_attn_a_norm_w"].reshape(16, 128).T,
        "W_out": full["L0_attn_a_W_out"],
        "Wq": full["L0_attn_b_Wq"], "Wk": full["L0_attn_b_Wk"],
        "Wv": full["L0_attn_b_Wv"], "Wo": full["L0_attn_b_Wo"],
        "bq_cols": full["L0_attn_b_bq"].reshape(16, 128).T,
        "bk_cols": full["L0_attn_b_bk"].reshape(16, 128).T,
        "bv_row": full["L0_attn_b_bv"].reshape(1, D),
        "bo_row": full["L0_attn_b_bo"].reshape(1, D),
        "w_gate": full["L0_ffn_w_gate"], "w_up": full["L0_ffn_w_up"],
        "w_down": full["L0_ffn_w_down"],
    }
    com = {k: np.ascontiguousarray(np.asarray(v, dtype=np.float32))
           for k, v in com.items()}
    com["neg_ones"] = np.full((1, D), -1.0, np.float32)
    com["ident_bf"] = np.eye(128, dtype=ml_dtypes.bfloat16)
    com["ident_f32"] = np.eye(128, dtype=np.float32)
    com["triu_bf"] = np.triu(np.ones((128, 128))).astype(ml_dtypes.bfloat16)
    maps = []
    for c in range(NCORES):
        m = dict(com)
        m["x"] = np.ascontiguousarray(x[c].astype(np.float32, copy=False))
        maps.append(m)
    return maps


def kernel(**inputs):
    maps = make_inputs(inputs)
    nc = build_nc(upto="all")
    res = run_bass_kernel_spmd(nc, maps, core_ids=list(range(NCORES)))
    return np.stack([np.asarray(r["out"], dtype=np.float32)
                     for r in res.results], axis=0)


# version 13
# speedup vs baseline: 1.0614x; 1.0614x over previous
import numpy as np
import ml_dtypes
from contextlib import ExitStack

import concourse.bass as bass
import concourse.mybir as mybir
from concourse.bass_utils import run_bass_kernel_spmd

F32 = mybir.dt.float32
BF16 = mybir.dt.bfloat16
AF = mybir.ActivationFunctionType
ALU = mybir.AluOpType
AX = mybir.AxisListType
AP = bass.AP

NCORES = 8
L = 4096
D = 2048
PROJ = 4224
NPO = 33
HBCT = 17
OC = 512
NOC = L // OC
FT = 512
NFP = L // FT
NH = 16
DH = 128
SCALE_QK = float(1.0 / np.sqrt(np.sqrt(128.0)))

ENGS = ("sync", "tensor", "vector", "scalar", "gpsimd")


SERIALIZE_DMA = [False]


class Rec:

    def __init__(self, nc, sems, cnt):
        self.nc = nc
        self.sems = sems
        self.cnt = cnt
        self.ops = {e: [] for e in ENGS}
        self.last_dma = {}

    def wait(self, eng, sem, val=None):
        v = self.cnt[sem] if val is None else val
        if v <= 0:
            return
        h = self.sems[sem]
        self.ops[eng].append(lambda e, h=h, v=v: e.wait_ge(h, v))

    def op(self, eng, fn, inc=None, n=1):
        if inc is not None:
            self.cnt[inc] += n
            h = self.sems[inc]
            self.ops[eng].append(lambda e, f=fn, h=h, n=n: f(e).then_inc(h, n))
            return self.cnt[inc]
        self.ops[eng].append(fn)
        return None

    def dma(self, eng, out, in_, inc=None, **kw):
        if inc is None:
            inc = {"sync": "dW", "gpsimd": "dG", "scalar": "dS",
                   "vector": "dV", "tensor": "dT"}[eng]
        if SERIALIZE_DMA[0] and eng in self.last_dma:
            ls, lv = self.last_dma[eng]
            self.wait(eng, ls, lv)
        self.cnt[inc] += 16
        self.last_dma[eng] = (inc, self.cnt[inc])
        h = self.sems[inc]
        self.ops[eng].append(
            lambda e, o=out, i=in_, h=h, kw=kw: e.dma_start(o, i, **kw).then_inc(h, 16))
        return self.cnt[inc]

    def fence(self):
        snap = [(s, self.cnt[s]) for s in self.sems]
        for e in ENGS:
            for s, v in snap:
                self.wait(e, s, v)

    def emit(self):
        with self.nc.Block() as blk:
            for e in ENGS:
                ops = self.ops[e]
                if ops:
                    getattr(blk, e)(lambda eng, ops=ops: [f(eng) for f in ops])
        self.ops = {e: [] for e in ENGS}



def build_nc(upto="all", debug=(), extern=(), m1dbg=False):
    nc = bass.Bass("TRN2", target_bir_lowering=False,
                   detect_race_conditions=False)

    di = {}

    def inp(name, shape, dt=F32):
        di[name] = nc.dram_tensor(name, list(shape), dt, kind="ExternalInput")
        return di[name]

    inp("x", (L, D))
    inp("W_in", (D, PROJ))
    inp("b_in_cols", (128, NPO))
    inp("conv_w_cols", (128, HBCT * 4))
    inp("conv_b_cols", (128, HBCT))
    inp("A_col", (32, 1))
    inp("dtb_col", (32, 1))
    inp("D_cols", (128, 16))
    inp("norm_cols", (128, 16))
    inp("W_out", (D, D))
    inp("Wq", (D, D)); inp("Wk", (D, D)); inp("Wv", (D, D)); inp("Wo", (D, D))
    inp("bq_cols", (128, 16)); inp("bk_cols", (128, 16))
    inp("bv_row", (1, D)); inp("bo_row", (1, D))
    inp("w_gate", (D, 8192)); inp("w_up", (D, 8192)); inp("w_down", (8192, D))
    inp("ident_bf", (128, 128), BF16)
    inp("ident_f32", (128, 128))
    inp("triu_bf", (128, 128), BF16)
    inp("neg_ones", (1, 2048))

    dr = {}
    def idram(name, shape, dt):
        k = "ExternalOutput" if name in extern else None
        if k:
            return nc.dram_tensor(name, list(shape), dt, kind=k)
        return nc.dram_tensor(name, list(shape), dt)
    dr["xbuf"] = idram("xbuf", [L, D], F32)
    dr["winb"] = idram("winb", [NPO * 16 * 128, 128], BF16)
    dr["woutb"] = idram("woutb", [D, D], BF16)
    dr["wqTb"] = idram("wqTb", [D, D], BF16)
    dr["wkTb"] = idram("wkTb", [D, D], BF16)
    dr["wvTb"] = idram("wvTb", [D, D], BF16)
    dr["woTb"] = idram("woTb", [D, D], BF16)
    dr["wgb"] = idram("wgb", [64 * 16 * 128, 128], BF16)
    dr["wub"] = idram("wub", [64 * 16 * 128, 128], BF16)
    dr["wdb"] = idram("wdb", [8192, D], BF16)
    dr["ynormb"] = idram("ynormb", [D, L], BF16)
    dr["qb_d"] = idram("qb_d", [D, L], BF16)
    dr["kb_d"] = idram("kb_d", [D, L], BF16)
    dr["vb_d"] = idram("vb_d", [NH * L, DH], BF16)
    dr["attnb"] = idram("attnb", [D, L], BF16)
    dr["bounce"] = idram("bounce", [64, 4096], F32)
    dr["bounce_bf"] = idram("bounce_bf", [16, 4096], BF16)
    if m1dbg:
        dr["mdbg"] = nc.dram_tensor("mdbg", [128, 8192], F32,
                                    kind="ExternalOutput")
    dr["out"] = nc.dram_tensor("out", [L, D], F32, kind="ExternalOutput")
    dbg = {}
    for name in debug:
        dbg[name] = nc.dram_tensor("dbg_" + name, [L, D], F32,
                                   kind="ExternalOutput")

    sem_names = ["dW", "dG", "dS", "dV", "dT", "tE", "vE", "sE", "gE",
                 "g0", "g1", "g2", "g3", "g4", "g5", "g6", "g7", "g8", "g9",
                 "gA", "gB", "gC", "gD"]

    with ExitStack() as st:
        sems = {s: st.enter_context(nc.semaphore(s)) for s in sem_names}
        cnt = {s: 0 for s in sem_names}

        C = {}

        def sb(name, shape, dt=F32):
            C[name] = st.enter_context(nc.sbuf_tensor("c_" + name, list(shape), dt))
            return C[name]

        sb("ident_b", (128, 128), BF16)
        sb("ident_f", (128, 128))
        sb("triu_b", (128, 128), BF16)
        sb("ones_cb", (128, 1), BF16)
        sb("ones_cf", (128, 1))
        sb("ones_rb", (1, 512), BF16)
        sb("ones_rf", (1, 512))
        sb("eps_ln", (128, 1))
        sb("eps_rms", (1, 1))
        sb("binc", (128, NPO))
        sb("cwc", (128, HBCT * 4))
        sb("cbc", (128, HBCT))
        sb("a_col", (32, 1))
        sb("dtb_col", (32, 1))
        sb("d_cols", (128, 16))
        sb("bq_c", (128, 16))
        sb("bk_c", (128, 16))
        sb("bv_r", (1, D), BF16)
        sb("bo_r", (1, D), BF16)

        P = dict(nc=nc, sems=sems, cnt=cnt, di=di, dr=dr, dbg=dbg, C=C,
                 m1dbg=m1dbg)

        phase_prep(P)
        if upto != "prep":
            phase_m1(P)
            phase_m2(P)
            dump_x(P, "x1")
        if upto in ("f1", "attn", "all"):
            phase_ffn(P, src=dr["xbuf"], dst=dr["xbuf"])
            dump_x(P, "x2")
        if upto in ("attn", "all"):
            phase_a1(P)
            phase_a2(P)
            phase_a3(P)
            dump_x(P, "x3")
        if upto == "all":
            phase_ffn(P, src=dr["xbuf"], dst=dr["out"])
        R = Rec(nc, sems, cnt)
        R.fence()
        R.emit()
    return nc


def dump_x(P, name):
    if name not in P["dbg"]:
        return
    R = Rec(P["nc"], P["sems"], P["cnt"])
    R.fence()
    for i in range(8):
        R.dma("gpsimd", P["dbg"][name][i * 512:(i + 1) * 512, :],
              P["dr"]["xbuf"][i * 512:(i + 1) * 512, :])
    R.fence()
    R.emit()


def emit_ln(R, P, xtile, lnt, lnsq, stat, glpos, dsem="dG"):
    C = P["C"]
    cnt = P["cnt"]
    R.wait("vector", dsem, glpos)
    R.op("vector", lambda e: e.reduce_sum(stat[:, 0:1], xtile[:, :], AX.X),
         inc="vE")
    R.op("vector", lambda e: e.tensor_mul(lnsq[:, :], xtile[:, :], xtile[:, :]))
    R.op("vector", lambda e: e.reduce_sum(stat[:, 1:2], lnsq[:, :], AX.X),
         inc="vE")
    R.wait("vector", "vE", cnt["vE"])
    R.op("vector", lambda e: e.tensor_scalar(
        stat[:, 2:3], stat[:, 0:1], 1.0 / D, None, ALU.mult), inc="vE")
    R.wait("vector", "vE", cnt["vE"])
    R.op("vector", lambda e: e.tensor_mul(stat[:, 3:4], stat[:, 2:3],
                                          stat[:, 2:3]))
    R.op("vector", lambda e: e.tensor_scalar(
        stat[:, 4:5], stat[:, 1:2], 1.0 / D, None, ALU.mult), inc="vE")
    R.wait("vector", "vE", cnt["vE"])
    R.op("vector", lambda e: e.tensor_sub(stat[:, 5:6], stat[:, 4:5],
                                          stat[:, 3:4]), inc="vE")
    R.wait("scalar", "vE", cnt["vE"])
    R.op("scalar", lambda e: e.activation(
        stat[:, 6:7], stat[:, 5:6], AF.Ln, bias=C["eps_ln"][:, :]))
    R.op("scalar", lambda e: e.activation(
        stat[:, 7:8], stat[:, 6:7], AF.Exp, scale=-0.5), inc="sE")
    R.wait("vector", "sE", cnt["sE"])
    R.op("vector", lambda e: e.tensor_scalar(
        lnsq[:, :], xtile[:, :], stat[:, 2:3], None, ALU.subtract))
    return R.op("vector", lambda e: e.tensor_scalar(
        lnt[:, :], lnsq[:, :], stat[:, 7:8], None, ALU.mult), inc="vE")


def emit_ln_transpose(R, P, lnt, lncm, psum, tt, width, nslots=4):
    cnt = P["cnt"]
    idf = P["C"]["ident_f"]
    R.wait("tensor", "vE", cnt["vE"])
    for ct in range(16):
        pp = ct % nslots
        R.op("tensor", lambda e, ct=ct, pp=pp, psum=psum: e.transpose(
            psum[:, pp * 512:pp * 512 + 128],
            lnt[:, ct * 128:(ct + 1) * 128], idf[:, :]), inc="tE")
        R.wait("scalar", "tE", cnt["tE"])
        R.op("scalar", lambda e, ct=ct, pp=pp, psum=psum: e.copy(
            lncm[:, ct * width + tt * 128:ct * width + (tt + 1) * 128],
            psum[:, pp * 512:pp * 512 + 128]), inc="sE")
        if ct >= nslots - 1:
            R.wait("tensor", "sE", cnt["sE"] - (nslots - 1))
    return cnt["sE"]



def phase_prep(P):
    nc, sems, cnt, di, dr, C = (P["nc"], P["sems"], P["cnt"], P["di"], P["dr"],
                                P["C"])
    R = Rec(nc, sems, cnt)
    with ExitStack() as st:
        wt = [st.enter_context(nc.sbuf_tensor("p_wt%d" % i, [128, 2048], F32))
              for i in range(4)]
        wb = [st.enter_context(nc.sbuf_tensor("p_wb%d" % i, [128, 2048], BF16))
              for i in range(4)]
        pb = st.enter_context(nc.sbuf_tensor("p_pb", [128, 8192], BF16))
        tb = st.enter_context(nc.sbuf_tensor("p_tb", [128, 2048], BF16))
        ncol = st.enter_context(nc.sbuf_tensor("p_ncol", [128, 16], F32))
        ps = st.enter_context(nc.psum_tensor("p_ps", [128, 2048], BF16))

        for nm, dst in (("ident_bf", "ident_b"), ("ident_f32", "ident_f"),
                        ("triu_bf", "triu_b"), ("b_in_cols", "binc"),
                        ("conv_w_cols", "cwc"), ("conv_b_cols", "cbc"),
                        ("dtb_col", "dtb_col"), ("D_cols", "d_cols"),
                        ("A_col", "a_col"), ("bq_cols", "bq_c"),
                        ("bk_cols", "bk_c")):
            R.dma("gpsimd", C[dst][:, :], di[nm][:, :])
        R.dma("gpsimd", ncol[:, :], di["norm_cols"][:, :])
        R.dma("gpsimd", wt[0][0:1, 0:D], di["bv_row"][:, :])
        g_loads = R.dma("gpsimd", wt[1][0:1, 0:D], di["bo_row"][:, :])
        for nm, v in (("ones_cb", 1.0), ("ones_cf", 1.0), ("ones_rb", 1.0),
                      ("ones_rf", 1.0), ("eps_ln", 1e-5), ("eps_rms", 1e-6)):
            R.op("vector", lambda e, nm=nm, v=v: e.memset(C[nm][:, :], v))
        R.wait("vector", "dG", g_loads)
        R.op("vector", lambda e: e.tensor_copy(C["bv_r"][:, :], wt[0][0:1, 0:D]))
        R.op("vector", lambda e: e.tensor_copy(C["bo_r"][:, :], wt[1][0:1, 0:D]),
             inc="vE")
        bvbo_v = cnt["vE"]
        R.wait("scalar", "dG", g_loads)
        R.op("scalar", lambda e: e.activation(C["a_col"][:, :], C["a_col"][:, :],
                                              AF.Exp))
        R.op("scalar", lambda e: e.mul(C["a_col"][:, :], C["a_col"][:, :], -1.0))
        R.op("scalar", lambda e: e.mul(C["bq_c"][:, :], C["bq_c"][:, :], SCALE_QK))
        R.op("scalar", lambda e: e.mul(C["bk_c"][:, :], C["bk_c"][:, :], SCALE_QK),
             inc="sE")

        state = {"i": 0, "wt_free": [bvbo_v, bvbo_v, 0, 0],
                 "wb_free": [0, 0, 0, 0],
                 "wt_sem": ["vE", "vE", "vE", "vE"]}
        STMAP = ["g0", "g1", "g4", "g5"]

        def cast_block(src_ap, scale=None, nw_ct=None, ceng="vector"):
            i = state["i"]; state["i"] += 1
            p = i % 4
            qeng = "sync" if p % 2 == 0 else "scalar"
            R.wait(qeng, state["wt_sem"][p], state["wt_free"][p])
            wl = R.dma(qeng, wt[p][:, :], src_ap, inc="g%d" % (6 + p))
            csem = "vE" if ceng == "vector" else "sE"
            R.wait(ceng, "g%d" % (6 + p), wl)
            R.wait(ceng, STMAP[p], state["wb_free"][p])
            if nw_ct is not None:
                v = R.op(ceng, lambda e, p=p, c=nw_ct: e.tensor_scalar(
                    wb[p][:, :], wt[p][:, :], ncol[:, c:c + 1], None, ALU.mult),
                    inc=csem)
            elif scale is not None:
                v = R.op(ceng, lambda e, p=p, s=scale: e.tensor_scalar(
                    wb[p][:, :], wt[p][:, :], float(s), None, ALU.mult),
                    inc=csem)
            elif ceng == "vector":
                v = R.op(ceng, lambda e, p=p: e.tensor_copy(
                    wb[p][:, :], wt[p][:, :]), inc=csem)
            else:
                v = R.op(ceng, lambda e, p=p: e.activation(
                    wb[p][:, :], wt[p][:, :], AF.Identity), inc=csem)
            state["wt_free"][p] = v
            state["wt_sem"][p] = csem
            return p, csem, v

        for j in range(NPO):
            src = AP(di["W_in"], j * 128,
                     [[PROJ, 128], [128 * PROJ, 16], [1, 128]])
            p, csem, v = cast_block(src)
            R.wait("gpsimd", csem, v)
            state["wb_free"][p] = R.dma(
                "gpsimd", dr["winb"][j * 2048:(j + 1) * 2048, :],
                AP(wb[p], 0, [[2048, 128], [128, 16], [1, 128]]),
                inc=STMAP[p])

        for ct in range(16):
            p, csem, v = cast_block(di["W_out"][ct * 128:(ct + 1) * 128, :],
                                    nw_ct=ct)
            R.wait("gpsimd", csem, v)
            state["wb_free"][p] = R.dma(
                "gpsimd", dr["woutb"][ct * 128:(ct + 1) * 128, :], wb[p][:, :],
                inc=STMAP[p])

        tb_free = 0
        prev_ev = {14: ("sE", 0), 15: ("vE", 0)}
        for (wnm, dnm, scale) in (("Wq", "wqTb", SCALE_QK),
                                  ("Wk", "wkTb", SCALE_QK),
                                  ("Wv", "wvTb", None), ("Wo", "woTb", None)):
            for i in range(16):
                p, csem, v = cast_block(di[wnm][i * 128:(i + 1) * 128, :],
                                        scale=scale)
                R.wait("tensor", csem, v)
                ev_done = {}
                for jj in range(16):
                    pp = jj % 2
                    ceng2 = "scalar" if jj % 2 == 0 else "vector"
                    if jj >= 2:
                        R.wait("tensor", ev_done[jj - 2][0], ev_done[jj - 2][1])
                    else:
                        R.wait("tensor", prev_ev[14 + jj][0],
                               prev_ev[14 + jj][1])
                    tpos = R.op(
                        "tensor", lambda e, p=p, jj=jj, pp=pp: e.transpose(
                            ps[:, pp * 1024:pp * 1024 + 128],
                            wb[p][:, jj * 128:(jj + 1) * 128],
                            C["ident_b"][:, :]), inc="tE")
                    R.wait(ceng2, "tE", tpos)
                    if jj <= 1:
                        R.wait(ceng2, "g2", tb_free)
                    if ceng2 == "scalar":
                        c = R.op("scalar", lambda e, jj=jj, pp=pp: e.copy(
                            tb[:, jj * 128:(jj + 1) * 128],
                            ps[:, pp * 1024:pp * 1024 + 128]), inc="sE")
                        ev_done[jj] = ("sE", c)
                    else:
                        c = R.op("vector", lambda e, jj=jj, pp=pp: e.tensor_copy(
                            tb[:, jj * 128:(jj + 1) * 128],
                            ps[:, pp * 1024:pp * 1024 + 128]), inc="vE")
                        ev_done[jj] = ("vE", c)
                state["wb_free"][p] = 0
                prev_ev = {14: ev_done[14], 15: ev_done[15]}
                R.wait("gpsimd", "sE", ev_done[14][1])
                R.wait("gpsimd", "vE", ev_done[15][1])
                dst = AP(dr[dnm], i * 128, [[D, 128], [128 * D, 16], [1, 128]])
                tb_free = R.dma("gpsimd", dst,
                                AP(tb, 0, [[2048, 128], [128, 16], [1, 128]]),
                                inc="g2")
                R.wait("sync", "tE", cnt["tE"])
                R.wait("scalar", "tE", cnt["tE"])

        for i in range(64):
            p, csem, v = cast_block(di["w_down"][i * 128:(i + 1) * 128, :])
            R.wait("gpsimd", csem, v)
            state["wb_free"][p] = R.dma(
                "gpsimd", dr["wdb"][i * 128:(i + 1) * 128, :], wb[p][:, :],
                inc=STMAP[p])

        pb_free = 0
        for (wnm, dnm) in (("w_gate", "wgb"), ("w_up", "wub")):
            for ct in range(16):
                for q in range(4):
                    p, csem, v = cast_block(
                        di[wnm][ct * 128:(ct + 1) * 128,
                                q * 2048:(q + 1) * 2048], ceng="scalar")
                    R.wait("vector", csem, v)
                    if q == 0:
                        R.wait("vector", "g3", pb_free)
                    for cb in range(4):
                        for j in range(4):
                            it = (4 * q + cb) * 4 + j
                            src = AP(wb[p], cb * 512 + j, [[2048, 128], [4, 128]])
                            R.op(
                                "vector", lambda e, src=src, it=it:
                                e.tensor_copy(pb[:, it * 128:(it + 1) * 128],
                                              src),
                                inc=("vE" if (cb == 3 and j == 3) else None))
                    state["wb_free"][p] = 0
                    state["wt_free"][p] = cnt["vE"]
                    state["wt_sem"][p] = "vE"
                R.wait("gpsimd", "vE", cnt["vE"])
                for g4 in range(4):
                    dst = AP(dr[dnm], g4 * 16 * 2048 * 128 + ct * 128,
                             [[2048, 128], [2048 * 128, 16], [1, 128]])
                    src = AP(pb, g4 * 16 * 128,
                             [[8192, 128], [128, 16], [1, 128]])
                    pb_free = R.dma("gpsimd", dst, src, inc="g3")
        R.fence()
        R.emit()



def phase_m1(P):
    nc, sems, cnt, di, dr, C = (P["nc"], P["sems"], P["cnt"], P["di"], P["dr"],
                                P["C"])
    R = Rec(nc, sems, cnt)
    with ExitStack() as st:
        def sba(name, shape, dt=F32):
            return st.enter_context(nc.sbuf_tensor(name, list(shape), dt))
        wsl = [sba("m1_w%d" % i, (128, 2048), BF16) for i in range(2)]
        xt = [sba("m1_x%d" % i, (128, 2048)) for i in range(2)]
        lnt = sba("m1_lnt", (128, 2048))
        lncm = sba("m1_lncm", (128, 16 * OC), BF16)
        gate = sba("m1_gate", (128, 16 * OC), BF16)
        hbc = sba("m1_hbc", (128, HBCT * (OC + 4)), BF16)
        conv = sba("m1_conv", (128, HBCT * OC))
        cacc = sba("m1_cacc", (128, OC))
        dts = sba("m1_dts", (32, OC))
        dtv = sba("m1_dtv", (32, OC))
        sp1 = sba("m1_sp1", (32, OC))
        cum = sba("m1_cum", (32, 128))
        ecum_b = sba("m1_ecb", (32, 128), BF16)
        ecum_fl = sba("m1_ecfl", (1, 2048), BF16)
        aend_v = sba("m1_aev", (32, 1))
        aend_c = sba("m1_aec", (32, 32))
        stagA = sba("m1_stA", (2, 2048))
        stagB = sba("m1_stB", (2, 2048))
        dab = sba("m1_dab", (128, 2048))
        dec = sba("m1_dec", (128, 2048), BF16)
        decm = sba("m1_decm", (128, 2048), BF16)
        mt = sba("m1_mt", (128, 2048), BF16)
        gt_b = sba("m1_gt", (128, 128), BF16)
        cpp = sba("m1_cpp", (32, 2048), BF16)
        xs_tm = sba("m1_xstm", (128, 2048), BF16)
        dt_tm = sba("m1_dttm", (128, 32))
        xp = sba("m1_xp", (128, 2048), BF16)
        xpp = sba("m1_xpp", (128, 2048), BF16)
        btm = sba("m1_btm", (128, 32), BF16)
        cols2 = sba("m1_c2", (128, 32))
        c0 = sba("m1_c0", (32, 128))
        hst = sba("m1_h", (32, 2048))
        hbf = sba("m1_hbf", (32, 2048), BF16)
        ysub = lnt
        ysq = dec
        ynb = xpp
        rstd_r = sba("m1_rsr", (1, 128))
        rstd_bc = sba("m1_rbc", (128, 128))
        stat = sba("m1_stat", (128, 8))
        P1 = st.enter_context(nc.psum_tensor("m1_P1", [128, 2048], F32))
        P2 = st.enter_context(nc.psum_tensor("m1_P2", [128, 2048], F32))

        idf = C["ident_f"]
        triu = C["triu_b"]

        R.op("vector", lambda e: e.memset(hst[:, :], 0.0))
        R.op("vector", lambda e: e.memset(hbf[:, :], 0.0))
        R.op("vector", lambda e: e.memset(stagA[0:1, :], 1.0))
        negones_g = R.dma("gpsimd", stagB[1:2, :], di["neg_ones"][0:1, :])
        for t in range(HBCT):
            R.op("vector", lambda e, t=t: e.memset(
                hbc[:, t * (OC + 4):t * (OC + 4) + 4], 0.0))
        R.op("vector", lambda e: e.memset(
            hbc[64:128, 16 * (OC + 4):17 * (OC + 4)], 0.0), inc="vE")

        xfree = [0, 0]
        ynb_g = [0]
        wfree = [0, 0]
        pevict = [0, 0]
        for oc in range(NOC):
            tok0 = oc * OC
            for tt in range(4):
                p = tt % 2
                R.wait("gpsimd", "vE", xfree[p])
                gl = R.dma("gpsimd", xt[p][:, :],
                           di["x"][tok0 + tt * 128:tok0 + (tt + 1) * 128, :],
                           inc="g%d" % p)
                if P["m1dbg"] and oc == 0 and tt == 0:
                    R.wait("gpsimd", "g0", gl)
                    R.dma("gpsimd", dr["mdbg"][:, 6272:6784], xt[0][:, 0:512])
                    R.wait("vector", "dG", cnt["dG"])
                xfree[p] = emit_ln(R, P, xt[p], lnt, dab, stat, gl,
                                   dsem="g%d" % p)
                if P["m1dbg"] and oc == 0 and tt == 0:
                    R.wait("gpsimd", "vE", xfree[p])
                    R.dma("gpsimd", dr["mdbg"][:, 6794:6802], stat[:, :])
                    R.dma("gpsimd", dr["mdbg"][:, 6802:7314], lnt[:, 0:512])
                    R.wait("vector", "dG", cnt["dG"])
                    R.wait("scalar", "dG", cnt["dG"])
                emit_ln_transpose(R, P, lnt, lncm, P2, tt, OC)
            R.wait("tensor", "sE", cnt["sE"])
            ln_done_s = cnt["sE"]

            if oc > 0:
                for t in range(HBCT):
                    o = t * (OC + 4)
                    R.op("vector", lambda e, o=o: e.tensor_copy(
                        hbc[:, o:o + 4], hbc[:, o + OC:o + OC + 4]),
                        inc=("vE" if t == HBCT - 1 else None))
            halo_v = cnt["vE"]

            p1_v_guard = cnt["vE"]
            for j in range(NPO):
                p = j % 2
                pj = j % 2
                R.wait("sync", "tE", wfree[p])
                wpos = R.dma("sync", wsl[p][:, :],
                             dr["winb"][j * 2048:(j + 1) * 2048, :],
                             inc="g%d" % (6 + p))
                R.wait("tensor", "g%d" % (6 + p), wpos)
                R.wait("tensor", "sE", ln_done_s)
                R.wait("tensor", "sE", pevict[pj])
                if j < 2:
                    R.wait("tensor", "vE", p1_v_guard)
                for ct in range(16):
                    R.op("tensor", lambda e, p=p, ct=ct, pj=pj: e.matmul(
                        P1[:, pj * 512:(pj + 1) * 512],
                        wsl[p][:, ct * 128:(ct + 1) * 128],
                        lncm[:, ct * OC:ct * OC + 512],
                        start=(ct == 0), stop=(ct == 15)),
                        inc=("tE" if ct == 15 else None))
                wfree[p] = cnt["tE"]
                R.wait("scalar", "tE", cnt["tE"])
                R.wait("scalar", "vE", halo_v)
                if j < 16:
                    R.op("scalar", lambda e, j=j, pj=pj: e.activation(
                        gate[:, j * OC:(j + 1) * OC],
                        P1[:, pj * 512:(pj + 1) * 512], AF.Silu,
                        bias=C["binc"][:, j:j + 1]), inc="sE")
                elif j < 32:
                    t = j - 16
                    R.op("scalar", lambda e, t=t, j=j, pj=pj: e.activation(
                        hbc[:, t * (OC + 4) + 4:t * (OC + 4) + 4 + OC],
                        P1[:, pj * 512:(pj + 1) * 512], AF.Identity,
                        bias=C["binc"][:, j:j + 1]), inc="sE")
                else:
                    R.op("scalar", lambda e, pj=pj: e.activation(
                        hbc[0:64, 16 * (OC + 4) + 4:16 * (OC + 4) + 4 + OC],
                        P1[0:64, pj * 512:pj * 512 + OC], AF.Identity,
                        bias=C["binc"][0:64, 32:33]), inc="sE")
                    R.op("scalar", lambda e, pj=pj: e.activation(
                        dts[:, 0:OC], P1[64:96, pj * 512:pj * 512 + OC],
                        AF.Identity, bias=C["binc"][64:96, 32:33]), inc="sE")
                pevict[pj] = cnt["sE"]
            proj_s = cnt["sE"]

            R.wait("vector", "sE", proj_s)
            for t in range(HBCT):
                o = t * (OC + 4)
                R.op("vector", lambda e, t=t, o=o: e.tensor_scalar(
                    cacc[:, :], hbc[:, o + 1:o + 1 + OC],
                    C["cwc"][:, t * 4:t * 4 + 1], None, ALU.mult))
                for k in (1, 2, 3):
                    R.op("vector", lambda e, t=t, o=o, k=k:
                         e.scalar_tensor_tensor(
                             cacc[:, :], hbc[:, o + 1 + k:o + 1 + k + OC],
                             C["cwc"][:, t * 4 + k:t * 4 + k + 1], cacc[:, :],
                             ALU.mult, ALU.add),
                         inc=("vE" if k == 3 else None))
                R.wait("scalar", "vE", cnt["vE"])
                R.op("scalar", lambda e, t=t: e.activation(
                    conv[:, t * OC:(t + 1) * OC], cacc[:, :], AF.Silu,
                    bias=C["cbc"][:, t:t + 1]), inc="sE")
                R.wait("vector", "sE", cnt["sE"])
            conv_s = cnt["sE"]

            R.op("scalar", lambda e: e.activation(
                sp1[:, :], dts[:, :], AF.Abs, bias=C["dtb_col"][:, :]))
            R.op("scalar", lambda e: e.activation(
                sp1[:, :], sp1[:, :], AF.Exp, scale=-1.0))
            R.op("scalar", lambda e: e.activation(
                sp1[:, :], sp1[:, :], AF.Ln, bias=1.0), inc="sE")
            R.wait("vector", "sE", cnt["sE"])
            R.op("vector", lambda e: e.tensor_scalar(
                dtv[:, :], dts[:, :], C["dtb_col"][:, :], None, ALU.add),
                inc="vE")
            R.wait("vector", "vE", cnt["vE"])
            R.op("vector", lambda e: e.tensor_scalar(
                dtv[:, :], dtv[:, :], 0.0, None, ALU.max), inc="vE")
            R.wait("vector", "vE", cnt["vE"])
            R.op("vector", lambda e: e.tensor_add(dtv[:, :], dtv[:, :],
                                                  sp1[:, :]), inc="vE")
            R.wait("vector", "vE", cnt["vE"])
            R.op("vector", lambda e: e.tensor_scalar(
                sp1[:, :], dtv[:, :], C["a_col"][:, :], None, ALU.mult),
                inc="vE")
            dt_v = cnt["vE"]

            for sc in range(4):
                s0 = sc * 128
                gtok = tok0 + s0
                bs = sc % 4
                R.wait("vector", "vE", dt_v)
                R.op("vector", lambda e, s0=s0: e.tensor_tensor_scan(
                    cum[:, :], sp1[:, s0:s0 + 128], sp1[:, s0:s0 + 128],
                    0.0, ALU.add, ALU.bypass), inc="vE")
                R.wait("scalar", "vE", cnt["vE"])
                R.op("scalar", lambda e: e.activation(
                    ecum_b[:, :], cum[:, :], AF.Exp))
                R.op("scalar", lambda e: e.activation(
                    aend_v[:, :], cum[:, 127:128], AF.Exp), inc="sE")
                ec_s = cnt["sE"]
                R.wait("gpsimd", "sE", ec_s)
                R.wait("gpsimd", "tE", cnt["tE"])
                R.dma("gpsimd", dr["bounce"][3 * bs, 0:4096],
                      AP(cum, 0, [[128, 32], [1, 128]]), inc="g2")
                R.dma("gpsimd", dr["bounce_bf"][bs, 0:4096],
                      AP(ecum_b, 0, [[128, 32], [1, 128]]), inc="g2")
                gw = R.dma("gpsimd", dr["bounce"][3 * bs + 1, 0:32],
                           AP(aend_v, 0, [[1, 32], [1, 1]]), inc="g2")
                R.wait("gpsimd", "g2", gw)
                flats_g = R.dma("gpsimd", aend_c[:, :],
                                AP(dr["bounce"], (3 * bs + 1) * 4096,
                                   [[0, 32], [1, 32]]), inc="g5")

                R.wait("tensor", "sE", conv_s)
                R.wait("tensor", "vE", cnt["vE"])
                for ct in range(16):
                    R.op("tensor", lambda e, ct=ct, s0=s0: e.transpose(
                        P2[:, ct * 128:(ct + 1) * 128],
                        conv[:, ct * OC + s0:ct * OC + s0 + 128], idf[:, :]))
                R.op("tensor", lambda e, s0=s0: e.transpose(
                    P1[:, 0:32], dtv[:, s0:s0 + 128], idf[0:32, 0:32]))
                R.op("tensor", lambda e, s0=s0: e.transpose(
                    P1[:, 32:64], conv[0:32, 16 * OC + s0:16 * OC + s0 + 128],
                    idf[0:32, 0:32]), inc="tE")
                R.wait("vector", "tE", cnt["tE"])
                R.op("vector", lambda e: e.tensor_copy(xs_tm[:, :], P2[:, :]))
                R.op("vector", lambda e: e.tensor_copy(dt_tm[:, :], P1[:, 0:32]))
                R.op("vector", lambda e: e.tensor_copy(btm[:, :], P1[:, 32:64]),
                     inc="vE")
                tr_v = cnt["vE"]
                R.op("vector", lambda e: e.tensor_tensor(
                    xp[:, :], xs_tm[:, :],
                    AP(dt_tm, 0, [[32, 128], [1, 32], [0, 64]]), ALU.mult),
                    inc="vE")
                xp_v = cnt["vE"]

                cv = R.op("vector", lambda e, s0=s0: e.tensor_copy(
                    c0[:, :], conv[32:64, 16 * OC + s0:16 * OC + s0 + 128]),
                    inc="vE")
                R.wait("tensor", "vE", cv)
                R.op("tensor", lambda e, s0=s0: e.matmul(
                    P1[:, 512:640],
                    conv[0:32, 16 * OC + s0:16 * OC + s0 + 128],
                    c0[:, :], start=True, stop=True), inc="tE")
                R.wait("scalar", "tE", cnt["tE"])
                R.op("scalar", lambda e: e.copy(gt_b[:, :], P1[:, 512:640]),
                     inc="sE")
                gt_s = cnt["sE"]

                for hf in range(2):
                    R.wait("gpsimd", "tE", cnt["tE"])
                    R.dma("gpsimd", stagA[1:2, :],
                          dr["bounce"][3 * bs, hf * 2048:(hf + 1) * 2048],
                          inc="g3")
                    R.dma("gpsimd", stagB[0:1, :],
                          dr["bounce"][3 * bs, hf * 2048:(hf + 1) * 2048],
                          inc="g3")
                    half_g = R.dma("gpsimd", ecum_fl[0:1, :],
                                   dr["bounce_bf"][bs,
                                                   hf * 2048:(hf + 1) * 2048],
                                   inc="g3")
                    R.wait("tensor", "g3", half_g)
                    R.wait("tensor", "g5", flats_g)
                    R.wait("tensor", "dG", negones_g)
                    if hf == 0:
                        R.wait("tensor", "vE", tr_v)
                    for h in range(16):
                        R.op("tensor", lambda e, h=h: e.matmul(
                            P2[:, h * 128:(h + 1) * 128],
                            stagA[0:2, h * 128:(h + 1) * 128],
                            stagB[0:2, h * 128:(h + 1) * 128],
                            start=True, stop=True),
                            inc=("tE" if h == 15 else None))
                    R.wait("scalar", "tE", cnt["tE"])
                    R.op("scalar", lambda e: e.activation(dab[:, :], P2[:, :],
                                                          AF.Abs))
                    R.op("scalar", lambda e: e.activation(
                        dec[:, :], dab[:, :], AF.Exp, scale=-1.0), inc="sE")
                    dec_s = cnt["sE"]
                    R.wait("vector", "sE", dec_s)
                    R.op("vector", lambda e: e.tensor_tensor(
                        decm[:, :], dec[:, :],
                        AP(triu, 0, [[128, 128], [0, 16], [1, 128]]), ALU.mult))
                    R.wait("vector", "sE", gt_s)
                    R.op("vector", lambda e: e.tensor_tensor(
                        mt[:, :], AP(gt_b, 0, [[128, 128], [0, 16], [1, 128]]),
                        decm[:, :], ALU.mult))
                    R.op("vector", lambda e, hf=hf: e.tensor_tensor(
                        cols2[:, hf * 16:(hf + 1) * 16],
                        dt_tm[:, hf * 16:(hf + 1) * 16],
                        AP(decm, 127, [[2048, 128], [128, 16]]), ALU.mult),
                        inc="vE")
                    R.wait("vector", "vE", cnt["vE"])
                    R.wait("vector", "g4", ynb_g[0])
                    R.op("vector", lambda e, hf=hf: e.tensor_tensor(
                        xpp[:, hf * 1024:(hf + 1) * 1024],
                        xs_tm[:, hf * 1024:(hf + 1) * 1024],
                        AP(cols2, hf * 16, [[32, 128], [1, 16], [0, 64]]),
                        ALU.mult), inc="vE")
                    half_v = cnt["vE"]
                    R.wait("tensor", "sE", dec_s)
                    for q in range(4):
                        R.op("tensor", lambda e, q=q: e.matmul(
                            P2[0:32, q * 512:(q + 1) * 512],
                            C["ones_rb"][0:1, 0:32],
                            ecum_fl[0:1, q * 512:(q + 1) * 512],
                            start=True, stop=True),
                            inc=("tE" if q == 3 else None))
                    R.wait("vector", "tE", cnt["tE"])
                    R.op("vector", lambda e, s0=s0: e.tensor_tensor(
                        cpp[:, :],
                        AP(conv, 32 * HBCT * OC + 16 * OC + s0,
                           [[HBCT * OC, 32], [0, 16], [1, 128]]),
                        P2[0:32, :], ALU.mult), inc="vE")
                    cpp_v = cnt["vE"]
                    R.wait("tensor", "vE", cpp_v)
                    R.wait("tensor", "vE", xp_v)
                    for h in range(16):
                        hh = hf * 16 + h
                        ct = hh // 2
                        b0 = (hh % 2) * 64
                        R.op("tensor", lambda e, hh=hh, h=h, ct=ct, b0=b0:
                             e.matmul(
                                 P1[b0:b0 + 64, ct * 128:(ct + 1) * 128],
                                 xp[:, hh * 64:(hh + 1) * 64],
                                 mt[:, h * 128:(h + 1) * 128],
                                 start=True, stop=False))
                        R.op("tensor", lambda e, hh=hh, h=h, ct=ct, b0=b0:
                             e.matmul(
                                 P1[b0:b0 + 64, ct * 128:(ct + 1) * 128],
                                 hbf[:, hh * 64:(hh + 1) * 64],
                                 cpp[:, h * 128:(h + 1) * 128],
                                 start=False, stop=True),
                             inc=("tE" if h == 15 else None))
                y_t = cnt["tE"]

                R.wait("tensor", "vE", cnt["vE"])
                for q in range(4):
                    R.op("tensor", lambda e, q=q: e.matmul(
                        P2[0:32, q * 512:(q + 1) * 512], btm[:, :],
                        xpp[:, q * 512:(q + 1) * 512], start=True, stop=True),
                        inc=("tE" if q == 3 else None))
                st_t = cnt["tE"]

                R.wait("vector", "tE", y_t)
                R.wait("vector", "g5", flats_g)
                for ct in range(16):
                    R.op("vector", lambda e, ct=ct, s0=s0:
                         e.scalar_tensor_tensor(
                             ysub[:, ct * 128:(ct + 1) * 128],
                             conv[:, ct * OC + s0:ct * OC + s0 + 128],
                             C["d_cols"][:, ct:ct + 1],
                             P1[:, ct * 128:(ct + 1) * 128], ALU.mult, ALU.add))
                for ct in range(16):
                    R.op("vector", lambda e, ct=ct, s0=s0: e.tensor_tensor(
                        ysub[:, ct * 128:(ct + 1) * 128],
                        ysub[:, ct * 128:(ct + 1) * 128],
                        gate[:, ct * OC + s0:ct * OC + s0 + 128], ALU.mult),
                        inc=("vE" if ct == 15 else None))
                R.wait("vector", "vE", cnt["vE"])
                R.op("vector", lambda e: e.tensor_mul(
                    ysq[:, :], ysub[:, :], ysub[:, :]), inc="vE")
                yev_v = cnt["vE"]
                if P["m1dbg"] and oc == 0 and sc == 0:
                    md = dr["mdbg"]
                    R.wait("gpsimd", "vE", yev_v)
                    R.dma("gpsimd", md[:, 0:2048], ysub[:, :])
                    R.dma("gpsimd", md[0:32, 2048:2560], dts[:, :])
                    R.dma("gpsimd", md[0:32, 2560:3072], dtv[:, :])
                    R.dma("gpsimd", md[0:32, 3072:3200], cum[:, :])
                    R.dma("gpsimd", md[:, 3200:3712], lncm[:, 0:512])
                    R.dma("gpsimd", md[0:2, 3712:5760], stagA[:, :])
                    R.dma("gpsimd", md[0:32, 5760:6272], sp1[:, :])
                    R.dma("gpsimd", md[0:32, 6784:6785], C["a_col"][:, :],
                          allow_slow_non_contiguous=True)
                    R.dma("gpsimd", md[0:32, 6785:6786], C["dtb_col"][:, :],
                          allow_slow_non_contiguous=True)
                    R.dma("gpsimd", md[:, 6786:6794], stat[:, :])
                    R.wait("vector", "dG", cnt["dG"])
                    R.wait("tensor", "dG", cnt["dG"])
                    R.wait("scalar", "dG", cnt["dG"])
                R.wait("vector", "tE", st_t)
                for hh in range(32):
                    R.op("vector", lambda e, hh=hh: e.scalar_tensor_tensor(
                        hst[:, hh * 64:(hh + 1) * 64],
                        hst[:, hh * 64:(hh + 1) * 64],
                        aend_c[:, hh:hh + 1], P2[0:32, hh * 64:(hh + 1) * 64],
                        ALU.mult, ALU.add))
                R.op("vector", lambda e: e.tensor_copy(hbf[:, :], hst[:, :]),
                     inc="vE")
                hupd_v = cnt["vE"]

                R.wait("tensor", "vE", yev_v)
                for ct in range(16):
                    R.op("tensor", lambda e, ct=ct: e.matmul(
                        P1[0:1, 0:128], C["ones_cb"][:, :],
                        ysq[:, ct * 128:(ct + 1) * 128],
                        start=(ct == 0), stop=(ct == 15)),
                        inc=("tE" if ct == 15 else None))
                R.wait("scalar", "tE", cnt["tE"])
                R.op("scalar", lambda e: e.activation(
                    rstd_r[:, :], P1[0:1, 0:128], AF.Ln, scale=1.0 / D,
                    bias=C["eps_rms"][:, :]))
                R.op("scalar", lambda e: e.activation(
                    rstd_r[:, :], rstd_r[:, :], AF.Exp, scale=-0.5), inc="sE")
                R.wait("gpsimd", "sE", cnt["sE"])
                g2w = R.dma("gpsimd", dr["bounce"][3 * bs + 2, 0:128],
                            rstd_r[0:1, :], inc="g2")
                R.wait("gpsimd", "g2", g2w)
                gw2 = R.dma("gpsimd", rstd_bc[:, :],
                            AP(dr["bounce"], (3 * bs + 2) * 4096,
                               [[0, 128], [1, 128]]), inc="g3")
                R.wait("vector", "g3", gw2)
                R.op("vector", lambda e: e.tensor_tensor(
                    ynb[:, :], ysub[:, :],
                    AP(rstd_bc, 0, [[128, 128], [0, 16], [1, 128]]), ALU.mult),
                    inc="vE")
                ynb_v = cnt["vE"]
                R.wait("gpsimd", "vE", ynb_v)
                for g4 in range(4):
                    dsty = AP(dr["ynormb"], (g4 * 4 * 128) * L + gtok,
                              [[L, 128], [128 * L, 4], [1, 128]])
                    srcy = AP(ynb, g4 * 4 * 128, [[2048, 128], [128, 4], [1, 128]])
                    ynb_g[0] = R.dma("gpsimd", dsty, srcy, inc="g4")
                R.wait("tensor", "vE", hupd_v)
                R.wait("tensor", "sE", cnt["sE"])
        R.fence()
        R.emit()



def phase_m2(P):
    nc, sems, cnt, di, dr = P["nc"], P["sems"], P["cnt"], P["di"], P["dr"]
    R = Rec(nc, sems, cnt)
    with ExitStack() as st:
        wsb = st.enter_context(nc.sbuf_tensor("m2_w", [128, 16 * 2048], BF16))
        yt = [st.enter_context(nc.sbuf_tensor("m2_y%d" % i, [128, 2048], BF16))
              for i in range(2)]
        xt = [st.enter_context(nc.sbuf_tensor("m2_x%d" % i, [128, 2048], F32))
              for i in range(2)]
        ot = [st.enter_context(nc.sbuf_tensor("m2_o%d" % i, [128, 2048], F32))
              for i in range(2)]
        PSq = [st.enter_context(nc.psum_tensor("m2_ps%d" % i, [128, 2048], F32))
               for i in range(2)]

        for ct in range(16):
            R.dma("sync", wsb[:, ct * 2048:(ct + 1) * 2048],
                  dr["woutb"][ct * 128:(ct + 1) * 128, :])
        w_d = cnt["dW"]
        yfree = [0, 0]
        ofree = [0, 0]
        xfree = [0, 0]
        psfree = [0, 0]
        for t in range(32):
            p = t % 2
            R.wait("gpsimd", "tE", yfree[p])
            gy = R.dma("gpsimd",
                       AP(yt[p], 0, [[2048, 128], [128, 16], [1, 128]]),
                       AP(dr["ynormb"], t * 128,
                          [[L, 128], [128 * L, 16], [1, 128]]), inc="g%d" % p)
            R.wait("gpsimd", "vE", xfree[p])
            gx = R.dma("gpsimd", xt[p][:, :],
                       di["x"][t * 128:(t + 1) * 128, :], inc="g%d" % (2 + p))
            R.wait("tensor", "dW", w_d)
            R.wait("tensor", "g%d" % p, gy)
            R.wait("tensor", "vE", psfree[p])
            for dg in range(4):
                for ct in range(16):
                    R.op("tensor", lambda e, p=p, dg=dg, ct=ct: e.matmul(
                        PSq[p][:, dg * 512:(dg + 1) * 512],
                        yt[p][:, ct * 128:(ct + 1) * 128],
                        wsb[:, ct * 2048 + dg * 512:ct * 2048 + (dg + 1) * 512],
                        start=(ct == 0), stop=(ct == 15)),
                        inc=("tE" if (dg == 3 and ct == 15) else None))
            yfree[p] = cnt["tE"]
            R.wait("vector", "tE", cnt["tE"])
            R.wait("vector", "g%d" % (2 + p), gx)
            R.wait("vector", "g%d" % (4 + p), ofree[p])
            psfree[p] = xfree[p] = R.op(
                "vector", lambda e, p=p: e.scalar_tensor_tensor(
                    ot[p][:, :], xt[p][:, :], 1.0, PSq[p][:, :],
                    ALU.mult, ALU.add), inc="vE")
            R.wait("gpsimd", "vE", cnt["vE"])
            ofree[p] = R.dma("gpsimd", dr["xbuf"][t * 128:(t + 1) * 128, :],
                             ot[p][:, :], inc="g%d" % (4 + p))
        R.fence()
        R.emit()



_FFN_N = [0]


def phase_ffn(P, src, dst):
    nc, sems, cnt, dr, C = P["nc"], P["sems"], P["cnt"], P["dr"], P["C"]
    R = Rec(nc, sems, cnt)
    tag = "f%d" % _FFN_N[0]
    _FFN_N[0] += 1
    with ExitStack() as st:
        def sba(name, shape, dt=F32):
            return st.enter_context(nc.sbuf_tensor(tag + name, list(shape), dt))
        xt = [sba("_x%d" % i, (128, 2048)) for i in range(2)]
        lnt = [sba("_lnt%d" % i, (128, 2048)) for i in range(2)]
        lnsq = sba("_lnsq", (128, 2048))
        lncm = sba("_lncm", (128, 16 * FT), BF16)
        hsp = sba("_hsp", (128, 64 * FT), BF16)
        wgt = [sba("_wgt%d" % i, (128, 2048), BF16) for i in range(2)]
        wut = [sba("_wut%d" % i, (128, 2048), BF16) for i in range(2)]
        gsil = sba("_gsil", (128, 512))
        hq = [sba("_hq%d" % i, (128, 512)) for i in range(4)]
        aq = [sba("_aq%d" % i, (128, 512)) for i in range(4)]
        tau = sba("_tau", (128, 512))
        msk = sba("_msk", (128, 512))
        t5 = sba("_t5", (128, 512))
        t6 = sba("_t6", (128, 512))
        wdt = [sba("_wd%d" % i, (128, 2048), BF16) for i in range(2)]
        xs2 = [sba("_xs%d" % i, (128, 512)) for i in range(2)]
        os2 = [sba("_os%d" % i, (128, 512)) for i in range(2)]
        stat = sba("_stat", (128, 8))
        PH = st.enter_context(nc.psum_tensor(tag + "_ph", [128, 2048], F32))
        PD = st.enter_context(nc.psum_tensor(tag + "_pd", [128, 2048], F32))

        xfree = [0, 0]
        xs2free = [0, 0]
        os2free = [0, 0]
        wfree = [0, 0]
        wdfree = [0, 0]
        phfree = [0, 0]
        hq_done = []
        for fp in range(NFP):
            tok0 = fp * FT
            R.wait("tensor", "vE", cnt["vE"])
            for tt in range(4):
                p = tt % 2
                R.wait("gpsimd", "vE", xfree[p])
                gl = R.dma("gpsimd", xt[p][:, :],
                           src[tok0 + tt * 128:tok0 + (tt + 1) * 128, :],
                           inc="g%d" % p)
                xfree[p] = emit_ln(R, P, xt[p], lnt[p], lnsq, stat, gl,
                                   dsem="g%d" % p)
                emit_ln_transpose(R, P, lnt[p], lncm, PD, tt, FT)
            R.wait("tensor", "sE", cnt["sE"])
            ln_s = cnt["sE"]

            for ic in range(16):
                for j in range(4):
                    it = 4 * ic + j
                    p = it % 2
                    ph0 = (it % 2) * 1024
                    R.wait("sync", "tE", wfree[p])
                    wg_pos = R.dma("sync", AP(wgt[p], 0,
                                              [[2048, 128], [128, 16], [1, 128]]),
                                   dr["wgb"][it * 2048:(it + 1) * 2048, :],
                                   inc="g%d" % (6 + p))
                    wu_pos = R.dma("sync", AP(wut[p], 0,
                                              [[2048, 128], [128, 16], [1, 128]]),
                                   dr["wub"][it * 2048:(it + 1) * 2048, :],
                                   inc="g%d" % (8 + p))
                    R.wait("tensor", "g%d" % (6 + p), wg_pos)
                    R.wait("tensor", "sE", ln_s)
                    R.wait("tensor", "vE", phfree[it % 2])
                    for ct in range(16):
                        R.op("tensor", lambda e, p=p, ct=ct, ph0=ph0: e.matmul(
                            PH[:, ph0:ph0 + 512],
                            wgt[p][:, ct * 128:(ct + 1) * 128],
                            lncm[:, ct * FT:ct * FT + 512],
                            start=(ct == 0), stop=(ct == 15)))
                    R.wait("tensor", "g%d" % (8 + p), wu_pos)
                    for ct in range(16):
                        R.op("tensor", lambda e, p=p, ct=ct, ph0=ph0: e.matmul(
                            PH[:, ph0 + 512:ph0 + 1024],
                            wut[p][:, ct * 128:(ct + 1) * 128],
                            lncm[:, ct * FT:ct * FT + 512],
                            start=(ct == 0), stop=(ct == 15)),
                            inc=("tE" if ct == 15 else None))
                    wfree[p] = cnt["tE"]
                    R.wait("scalar", "tE", cnt["tE"])
                    if hq_done:
                        R.wait("scalar", "vE", hq_done[-1])
                    R.op("scalar", lambda e, ph0=ph0: e.activation(
                        gsil[:, :], PH[:, ph0:ph0 + 512], AF.Silu), inc="sE")
                    R.wait("vector", "sE", cnt["sE"])
                    hq_done.append(R.op(
                        "vector", lambda e, j=j, ph0=ph0: e.tensor_tensor(
                            hq[j][:, :], gsil[:, :], PH[:, ph0 + 512:ph0 + 1024],
                            ALU.mult), inc="vE"))
                    phfree[it % 2] = cnt["vE"]
                    R.wait("scalar", "vE", cnt["vE"])
                    R.op("scalar", lambda e, j=j: e.activation(
                        aq[j][:, :], hq[j][:, :], AF.Abs), inc="sE")
                abs_s = cnt["sE"]
                R.wait("vector", "sE", abs_s)
                R.op("vector", lambda e: e.tensor_max(tau[:, :], aq[0][:, :],
                                                      aq[1][:, :]))
                R.op("vector", lambda e: e.tensor_tensor(
                    msk[:, :], aq[0][:, :], aq[1][:, :], ALU.min))
                R.op("vector", lambda e: e.tensor_max(t5[:, :], aq[2][:, :],
                                                      aq[3][:, :]))
                R.op("vector", lambda e: e.tensor_tensor(
                    t6[:, :], aq[2][:, :], aq[3][:, :], ALU.min))
                R.op("vector", lambda e: e.tensor_tensor(
                    tau[:, :], tau[:, :], t5[:, :], ALU.min))
                R.op("vector", lambda e: e.tensor_max(msk[:, :], msk[:, :],
                                                      t6[:, :]), inc="vE")
                R.wait("vector", "vE", cnt["vE"])
                R.op("vector", lambda e: e.tensor_max(tau[:, :], tau[:, :],
                                                      msk[:, :]), inc="vE")
                R.wait("vector", "vE", cnt["vE"])
                for j in range(4):
                    it = 4 * ic + j
                    R.op("vector", lambda e, j=j: e.tensor_tensor(
                        msk[:, :], aq[j][:, :], tau[:, :], ALU.is_ge))
                    R.op("vector", lambda e, j=j, it=it: e.tensor_tensor(
                        hsp[:, it * FT:(it + 1) * FT], hq[j][:, :],
                        msk[:, :], ALU.mult),
                        inc=("vE" if j == 3 else None))
            hsp_v = cnt["vE"]

            pd_ev = [0, 0, 0, 0]
            for cg in range(4):
                for gg in range(16):
                    p = gg % 2
                    R.wait("sync", "tE", wdfree[p])
                    wd_pos = R.dma(
                        "sync", AP(wdt[p], 0, [[2048, 128], [512, 4], [1, 512]]),
                        AP(dr["wdb"], 512 * gg * D + cg * 512,
                           [[4 * D, 128], [D, 4], [1, 512]]),
                        inc="g%d" % (2 + p))
                    R.wait("tensor", "g%d" % (2 + p), wd_pos)
                    if gg == 0:
                        R.wait("tensor", "vE", hsp_v)
                    for j4 in range(4):
                        kt = 4 * gg + j4
                        for tt in range(4):
                            if cg > 0 and kt == 0:
                                R.wait("tensor", "vE", pd_ev[tt])
                            R.op("tensor",
                                 lambda e, p=p, kt=kt, tt=tt, j4=j4: e.matmul(
                                     PD[:, tt * 512:(tt + 1) * 512],
                                     hsp[:, kt * FT + tt * 128:
                                         kt * FT + (tt + 1) * 128],
                                     wdt[p][:, j4 * 512:(j4 + 1) * 512],
                                     start=(kt == 0), stop=(kt == 63)),
                                 inc=("tE" if (j4 == 3 and tt == 3) else None))
                    wdfree[p] = cnt["tE"]
                R.wait("vector", "tE", cnt["tE"])
                for tt in range(4):
                    pp = tt % 2
                    R.wait("sync", "vE", xs2free[pp])
                    gx = R.dma("sync", xs2[pp][:, :],
                               AP(src, (tok0 + tt * 128) * D + cg * 512,
                                  [[D, 128], [1, 512]]),
                               inc=("gA" if pp == 0 else "gB"))
                    R.wait("vector", "gA" if pp == 0 else "gB", gx)
                    R.wait("vector", "g%d" % (4 + pp), os2free[pp])
                    xs2free[pp] = R.op(
                        "vector", lambda e, pp=pp, tt=tt:
                        e.scalar_tensor_tensor(
                            os2[pp][:, :], xs2[pp][:, :], 1.0,
                            PD[:, tt * 512:(tt + 1) * 512],
                            ALU.mult, ALU.add), inc="vE")
                    pd_ev[tt] = xs2free[pp]
                    R.wait("gpsimd", "vE", cnt["vE"])
                    os2free[pp] = R.dma(
                        "gpsimd", AP(dst, (tok0 + tt * 128) * D + cg * 512,
                                     [[D, 128], [1, 512]]), os2[pp][:, :],
                        inc="g%d" % (4 + pp))
        R.fence()
        R.emit()



def phase_a1(P):
    nc, sems, cnt, dr, C = P["nc"], P["sems"], P["cnt"], P["dr"], P["C"]
    R = Rec(nc, sems, cnt)
    with ExitStack() as st:
        def sba(name, shape, dt=F32):
            return st.enter_context(nc.sbuf_tensor(name, list(shape), dt))
        xt = [sba("a1_x%d" % i, (128, 2048)) for i in range(2)]
        lnt = sba("a1_lnt", (128, 2048))
        lnsq = sba("a1_lnsq", (128, 2048))
        lncm = sba("a1_lncm", (128, 16 * 2048), BF16)
        wq4 = [sba("a1_w%d" % i, (128, 16 * 512), BF16) for i in range(2)]
        ev = [sba("a1_e%d" % i, (128, 512), BF16) for i in range(2)]
        stat = sba("a1_stat", (128, 8))
        PS = st.enter_context(nc.psum_tensor("a1_ps", [128, 1024], F32))

        xfree = [0, 0]
        wfree = [0, 0]
        ev_sfree = [0, 0]
        ev_gfree = [0, 0]
        evfree2 = [0, 0]
        for hs in range(2):
            t0 = hs * 2048
            R.wait("tensor", "vE", cnt["vE"])
            for tt in range(16):
                p = tt % 2
                R.wait("gpsimd", "vE", xfree[p])
                gl = R.dma("gpsimd", xt[p][:, :],
                           dr["xbuf"][t0 + tt * 128:t0 + (tt + 1) * 128, :],
                           inc="g%d" % p)
                xfree[p] = emit_ln(R, P, xt[p], lnt, lnsq, stat, gl,
                                   dsem="g%d" % p)
                emit_ln_transpose(R, P, lnt, lncm, PS, tt, 2048, nslots=2)
            R.wait("tensor", "sE", cnt["sE"])
            ln_s = cnt["sE"]

            for (wd, dst_d, bc) in ((dr["wqTb"], dr["qb_d"], C["bq_c"]),
                                    (dr["wkTb"], dr["kb_d"], C["bk_c"])):
                for q in range(4):
                    p = q % 2
                    R.wait("sync", "tE", wfree[p])
                    wpos = R.dma("sync",
                                 AP(wq4[p], 0, [[8192, 128], [512, 16], [1, 512]]),
                                 AP(wd, q * 512, [[D, 128], [128 * D, 16],
                                                  [1, 512]]),
                                 inc="g%d" % (6 + p))
                    for dt4 in range(4):
                        dpt = q * 4 + dt4
                        for tc in range(4):
                            pe = (dt4 * 4 + tc) % 2
                            R.wait("tensor", "g%d" % (6 + p), wpos)
                            R.wait("tensor", "sE", ln_s)
                            R.wait("tensor", "sE", ev_sfree[pe])
                            R.wait("tensor", "g%d" % (2 + pe), evfree2[pe])
                            for ct in range(16):
                                R.op("tensor", lambda e, p=p, ct=ct, dt4=dt4,
                                     tc=tc, pe=pe: e.matmul(
                                         PS[:, pe * 512:(pe + 1) * 512],
                                         wq4[p][:, ct * 512 + dt4 * 128:
                                                ct * 512 + (dt4 + 1) * 128],
                                         lncm[:, ct * 2048 + tc * 512:
                                              ct * 2048 + (tc + 1) * 512],
                                         start=(ct == 0), stop=(ct == 15)),
                                     inc=("tE" if ct == 15 else None))
                            R.wait("scalar", "tE", cnt["tE"])
                            R.wait("scalar", "g%d" % (4 + pe), ev_gfree[pe])
                            ev_sfree[pe] = R.op(
                                "scalar", lambda e, pe=pe, dpt=dpt, bc=bc:
                                e.activation(
                                    ev[pe][:, :], PS[:, pe * 512:(pe + 1) * 512],
                                    AF.Identity, bias=bc[:, dpt:dpt + 1]), inc="sE")
                            R.wait("gpsimd", "sE", cnt["sE"])
                            ev_gfree[pe] = R.dma(
                                "gpsimd",
                                AP(dst_d, dpt * 128 * L + t0 + tc * 512,
                                   [[L, 128], [1, 512]]), ev[pe][:, :],
                                inc="g%d" % (4 + pe))
                    wfree[p] = cnt["tE"]

            for q in range(4):
                p = q % 2
                R.wait("sync", "tE", wfree[p])
                wpos = R.dma("sync",
                             AP(wq4[p], 0, [[8192, 128], [512, 16], [1, 512]]),
                             AP(dr["wvTb"], q * 512,
                                [[D, 128], [128 * D, 16], [1, 512]]),
                             inc="g%d" % (6 + p))
                for tt in range(16):
                    pe = tt % 2
                    R.wait("tensor", "g%d" % (6 + p), wpos)
                    R.wait("tensor", "sE", ln_s)
                    R.wait("tensor", "g%d" % (2 + pe), evfree2[pe])
                    R.wait("tensor", "sE", ev_sfree[pe])
                    for ct in range(16):
                        R.op("tensor", lambda e, p=p, ct=ct, tt=tt, pe=pe:
                             e.matmul(
                                 PS[:, pe * 512:(pe + 1) * 512],
                                 lncm[:, ct * 2048 + tt * 128:
                                      ct * 2048 + (tt + 1) * 128],
                                 wq4[p][:, ct * 512:(ct + 1) * 512],
                                 start=(ct == 0), stop=False))
                    R.op("tensor", lambda e, pe=pe, q=q: e.matmul(
                        PS[:, pe * 512:(pe + 1) * 512],
                        C["ones_rb"][0:1, 0:128],
                        C["bv_r"][0:1, q * 512:(q + 1) * 512],
                        start=False, stop=True), inc="tE")
                    R.wait("vector", "tE", cnt["tE"])
                    R.wait("vector", "g%d" % (2 + pe), evfree2[pe])
                    evfree2[pe] = R.op(
                        "vector", lambda e, pe=pe: e.tensor_copy(
                            ev[pe][:, :], PS[:, pe * 512:(pe + 1) * 512]),
                        inc="vE")
                    R.wait("gpsimd", "vE", cnt["vE"])
                    for hh4 in range(4):
                        h = q * 4 + hh4
                        evfree2[pe] = R.dma(
                            "gpsimd",
                            dr["vb_d"][h * L + t0 + tt * 128:
                                       h * L + t0 + (tt + 1) * 128, :],
                            ev[pe][:, hh4 * 128:(hh4 + 1) * 128],
                            inc="g%d" % (2 + pe))
                wfree[p] = cnt["tE"]
        R.fence()
        R.emit()


def phase_a2(P):
    nc, sems, cnt, dr, C = P["nc"], P["sems"], P["cnt"], P["dr"], P["C"]
    R = Rec(nc, sems, cnt)
    with ExitStack() as st:
        def sba(name, shape, dt=F32):
            return st.enter_context(nc.sbuf_tensor(name, list(shape), dt))
        qh = [sba("a2_q%d" % i, (128, L), BF16) for i in range(2)]
        kh = [sba("a2_k%d" % i, (128, L), BF16) for i in range(2)]
        vh = [sba("a2_v%d" % i, (128, 32 * 128), BF16) for i in range(2)]
        p4s = [sba("a2_p4s%d" % i, (128, 32 * 512), BF16) for i in range(2)]
        pt = [sba("a2_pt%d" % i, (128, 512), BF16) for i in range(4)]
        sq = [sba("a2_sq%d" % i, (128, 512), BF16) for i in range(4)]
        zb = sba("a2_zb", (128, 512), BF16)
        mrow = sba("a2_mrow", (1, 512))
        z_b = sba("a2_zrow", (1, 512), BF16)
        oa = [sba("a2_oa%d" % i, (128, 512), BF16) for i in range(2)]
        PSA = st.enter_context(nc.psum_tensor("a2_psa", [128, 2048], F32))
        PO = st.enter_context(nc.psum_tensor("a2_po", [128, 1024], F32))
        PCS = st.enter_context(nc.psum_tensor("a2_pcs", [1, 512], F32))

        hfree = [0, 0]
        po_free = [0, 0]
        pcs_free = [0]
        psa_free = [0, 0, 0, 0]
        s4_guard = [0]
        s4_dma = [0]
        zrow_guard = [0]
        oa_store = [0, 0]
        sqfree = [0, 0, 0, 0]
        sq_done = {}
        ptfree = [0, 0, 0, 0]
        p4read = [[0] * 32, [0] * 32]
        gv_p = [0, 0]
        LA = 3
        jobs = [(h, qb) for h in range(NH) for qb in range(8)]
        exp_sd = {}
        cs_last = {}
        ptv = {}
        pvt = {}
        s4s = {}

        def emit_loads(h):
            p = h % 2
            R.wait("gpsimd", "tE", hfree[p])
            R.dma("gpsimd", qh[p][:, :], dr["qb_d"][h * 128:(h + 1) * 128, :],
                  inc="g%d" % p)
            R.dma("gpsimd", kh[p][:, :], dr["kb_d"][h * 128:(h + 1) * 128, :],
                  inc="g%d" % p)
            for q4 in range(4):
                gv_p[p] = R.dma(
                    "gpsimd",
                    AP(vh[p], q4 * 8 * 128, [[4096, 128], [128, 8], [1, 128]]),
                    AP(dr["vb_d"], (h * L + q4 * 1024) * DH,
                       [[DH, 128], [128 * DH, 8], [1, DH]]), inc="g%d" % p)

        def emit_pass1_pair(i, kt):
            h, qb = jobs[i]
            p = h % 2
            pf = i % 2
            pe = kt % 4
            if kt == 0:
                R.wait("tensor", "g%d" % p, gv_p[p])
            R.wait("tensor", "sE", max(psa_free[pe], psa_free[pe + 1]))
            R.op("tensor", lambda e, p=p, kt=kt, qb=qb, pe=pe:
                 e.matmul(PSA[:, pe * 512:(pe + 1) * 512],
                          kh[p][:, kt * 128:(kt + 1) * 128],
                          qh[p][:, qb * 512:(qb + 1) * 512],
                          start=True, stop=True))
            mm_t = R.op("tensor", lambda e, p=p, kt=kt + 1, qb=qb, pe=pe + 1:
                        e.matmul(PSA[:, pe * 512:(pe + 1) * 512],
                                 kh[p][:, kt * 128:(kt + 1) * 128],
                                 qh[p][:, qb * 512:(qb + 1) * 512],
                                 start=True, stop=True), inc="tE")
            R.wait("scalar", "tE", mm_t)
            R.wait("scalar", "vE", max(p4read[pf][kt], p4read[pf][kt + 1]))
            R.op("scalar", lambda e, pe=pe, kt=kt, pf=pf: e.activation(
                p4s[pf][:, kt * 512:(kt + 1) * 512],
                PSA[:, pe * 512:(pe + 1) * 512],
                AF.Exp, scale=0.5))
            s = R.op("scalar", lambda e, pe=pe + 1, kt=kt + 1, pf=pf:
                     e.activation(p4s[pf][:, kt * 512:(kt + 1) * 512],
                                  PSA[:, pe * 512:(pe + 1) * 512],
                                  AF.Exp, scale=0.5), inc="sE")
            exp_sd[i][kt] = s
            exp_sd[i][kt + 1] = s
            psa_free[pe] = s
            psa_free[pe + 1] = s
            for k2 in (kt, kt + 1):
                sl = k2 % 4
                r = k2 - 4 * qb
                if r >= 0:
                    R.wait("vector", "sE", s)
                    R.wait("vector", "tE", sqfree[sl])
                    if r > 0:
                        R.op("vector", lambda e, pf=pf, kt=k2, r=r: e.memset(
                            p4s[pf][:, kt * 512:kt * 512 + r * 128], 0.0))
                    R.op("vector", lambda e, pf=pf, kt=k2, r=r: e.tensor_tensor(
                        p4s[pf][:, kt * 512 + r * 128:kt * 512 + (r + 1) * 128],
                        p4s[pf][:, kt * 512 + r * 128:kt * 512 + (r + 1) * 128],
                        C["triu_b"][:, :], ALU.mult))
                    c = R.op("vector", lambda e, pf=pf, kt=k2, sl=sl:
                             e.tensor_tensor(
                                 sq[sl][:, :],
                                 p4s[pf][:, kt * 512:(kt + 1) * 512],
                                 p4s[pf][:, kt * 512:(kt + 1) * 512],
                                 ALU.mult), inc="vE")
                    sq_done[k2] = ("vE", c)
                elif k2 % 2 == 0:
                    R.wait("scalar", "tE", sqfree[sl])
                    c = R.op("scalar", lambda e, pf=pf, kt=k2, sl=sl:
                             e.activation(sq[sl][:, :],
                                          p4s[pf][:, kt * 512:(kt + 1) * 512],
                                          AF.Square), inc="sE")
                    sq_done[k2] = ("sE", c)
                else:
                    R.wait("vector", "sE", s)
                    R.wait("vector", "tE", sqfree[sl])
                    c = R.op("vector", lambda e, pf=pf, kt=k2, sl=sl:
                             e.tensor_tensor(
                                 sq[sl][:, :],
                                 p4s[pf][:, kt * 512:(kt + 1) * 512],
                                 p4s[pf][:, kt * 512:(kt + 1) * 512],
                                 ALU.mult), inc="vE")
                    sq_done[k2] = ("vE", c)

        def emit_cs_pair(i, j):
            h, qb = jobs[i]
            kt_hi = 4 * qb + 4
            R.wait("tensor", sq_done[j][0], sq_done[j][1])
            R.wait("tensor", sq_done[j + 1][0], sq_done[j + 1][1])
            if j == 0:
                R.wait("tensor", "sE", pcs_free[0])
            R.op("tensor", lambda e, sl=j % 4:
                 e.matmul(PCS[:, :], C["ones_cb"][:, :], sq[sl][:, :],
                          start=(j == 0), stop=False))
            c = R.op("tensor", lambda e, sl=(j + 1) % 4, last=(j + 1 == kt_hi - 1):
                     e.matmul(PCS[:, :], C["ones_cb"][:, :], sq[sl][:, :],
                              start=False, stop=last), inc="tE")
            sqfree[j % 4] = c
            sqfree[(j + 1) % 4] = c
            if j + 1 == kt_hi - 1:
                cs_last[i] = c

        def emit_between(i):
            R.wait("scalar", "tE", cs_last[i])
            pcs_free[0] = R.op("scalar", lambda e: e.activation(
                mrow[:, :], PCS[:, :], AF.Ln), inc="sE")
            R.wait("scalar", "gD", zrow_guard[0])
            zs = R.op("scalar", lambda e: e.activation(
                z_b[:, :], mrow[:, :], AF.Exp, scale=-0.5), inc="sE")
            R.wait("gpsimd", "sE", zs)
            R.wait("gpsimd", "vE", s4_guard[0])
            d1 = R.dma("gpsimd", dr["bounce_bf"][12, 0:512], z_b[0:1, :],
                       inc="gD")
            zrow_guard[0] = d1
            R.wait("gpsimd", "gD", d1)
            s4_dma[0] = R.dma("gpsimd", zb[:, :],
                              AP(dr["bounce_bf"], 12 * 4096,
                                 [[0, 128], [1, 512]]), inc="gD")
            s4s[i] = s4_dma[0]

        def emit_pass2_vec_pair(i, kt):
            h, qb = jobs[i]
            kt_hi = 4 * qb + 4
            pf = i % 2
            pe = kt % 4
            if kt == 0:
                R.wait("vector", "gD", s4s[i])
            R.wait("vector", "tE", max(ptfree[pe], ptfree[pe + 1]))
            v = 0
            for k2 in (kt, kt + 1):
                pe2 = k2 % 4
                R.op("vector", lambda e, pe=pe2, kt=k2, pf=pf: e.tensor_tensor(
                    pt[pe][:, :], p4s[pf][:, kt * 512:(kt + 1) * 512],
                    zb[:, :], ALU.mult))
                v = R.op("vector", lambda e, pe=pe2: e.tensor_tensor(
                    pt[pe][:, :], pt[pe][:, :], pt[pe][:, :], ALU.mult),
                    inc=("vE" if k2 == kt + 1 else None))
            ptv[i][kt] = v
            ptv[i][kt + 1] = v
            p4read[pf][kt] = v
            p4read[pf][kt + 1] = v
            if kt + 1 == kt_hi - 1:
                s4_guard[0] = v

        def emit_pass2_mm_pair(i, j):
            h, qb = jobs[i]
            kt_hi = 4 * qb + 4
            p = h % 2
            op = i % 2
            R.wait("tensor", "vE", ptv[i][j + 1])
            if j == 0:
                R.wait("tensor", "sE", po_free[op])
            t = 0
            for j2 in (j, j + 1):
                pe2 = j2 % 4
                t = R.op("tensor", lambda e, p=p, j=j2, pe=pe2, op=op,
                         kt_hi=kt_hi: e.matmul(
                             PO[:, op * 512:(op + 1) * 512],
                             vh[p][:, j * 128:(j + 1) * 128],
                             pt[pe][:, :], start=(j == 0),
                             stop=(j == kt_hi - 1)),
                         inc=("tE" if j2 == j + 1 else None))
            ptfree[j % 4] = t
            ptfree[(j + 1) % 4] = t
            pvt[i][j] = t
            pvt[i][j + 1] = t

        def emit_epilogue(i):
            h, qb = jobs[i]
            kt_hi = 4 * qb + 4
            op = i % 2
            R.wait("scalar", "tE", pvt[i][kt_hi - 1])
            R.wait("scalar", "g4", oa_store[op])
            c = R.op("scalar", lambda e, op=op: e.activation(
                oa[op][:, :], PO[:, op * 512:(op + 1) * 512], AF.Identity),
                inc="sE")
            po_free[op] = c
            R.wait("gpsimd", "sE", c)
            oa_store[op] = R.dma(
                "gpsimd", AP(dr["attnb"], h * 128 * L + qb * 512,
                             [[L, 128], [1, 512]]), oa[op][:, :], inc="g4")
            if qb == 7:
                hfree[h % 2] = cnt["tE"]

        LAP = 2
        emit_loads(0)
        exp_sd[0] = {}
        for step in range(2 + LAP):
            if 0 <= step - LAP < 2:
                emit_cs_pair(0, 2 * (step - LAP))
            if step < 2:
                emit_pass1_pair(0, 2 * step)
        for i in range(len(jobs)):
            h, qb = jobs[i]
            if qb == 4 and h + 1 < NH:
                emit_loads(h + 1)
            emit_between(i)
            n2 = (4 * qb + 4) // 2
            i1 = i + 1 if i + 1 < len(jobs) else None
            n1 = ((4 * jobs[i1][1] + 4) // 2) if i1 is not None else 0
            if i1 is not None:
                exp_sd[i1] = {}
            ptv[i] = {}
            pvt[i] = {}
            for step in range(max(n1, n2) + LAP):
                if i1 is not None and 0 <= step - LAP < n1:
                    emit_cs_pair(i1, 2 * (step - LAP))
                if i1 is not None and step < n1:
                    emit_pass1_pair(i1, 2 * step)
                if 0 <= step - LAP < n2:
                    emit_pass2_mm_pair(i, 2 * (step - LAP))
                if step < n2:
                    emit_pass2_vec_pair(i, 2 * step)
            emit_epilogue(i)
        R.fence()
        R.emit()


def phase_a3(P):
    nc, sems, cnt, dr, C = P["nc"], P["sems"], P["cnt"], P["dr"], P["C"]
    R = Rec(nc, sems, cnt)
    with ExitStack() as st:
        wsb = st.enter_context(nc.sbuf_tensor("a3_w", [128, 16 * 2048], BF16))
        at = [st.enter_context(nc.sbuf_tensor("a3_a%d" % i, [128, 2048], BF16))
              for i in range(2)]
        xt = [st.enter_context(nc.sbuf_tensor("a3_x%d" % i, [128, 2048], F32))
              for i in range(2)]
        ot = [st.enter_context(nc.sbuf_tensor("a3_o%d" % i, [128, 2048], F32))
              for i in range(2)]
        PSq = [st.enter_context(nc.psum_tensor("a3_ps%d" % i, [128, 2048], F32))
               for i in range(2)]

        for ct in range(16):
            R.dma("sync", wsb[:, ct * 2048:(ct + 1) * 2048],
                  dr["woTb"][ct * 128:(ct + 1) * 128, :])
        w_d = cnt["dW"]
        afree = [0, 0]
        xfree = [0, 0]
        ofree = [0, 0]
        psfree = [0, 0]
        for t in range(32):
            p = t % 2
            R.wait("gpsimd", "tE", afree[p])
            ga = R.dma("gpsimd",
                       AP(at[p], 0, [[2048, 128], [128, 16], [1, 128]]),
                       AP(dr["attnb"], t * 128,
                          [[L, 128], [128 * L, 16], [1, 128]]), inc="g%d" % p)
            R.wait("gpsimd", "vE", xfree[p])
            gx = R.dma("gpsimd", xt[p][:, :],
                       dr["xbuf"][t * 128:(t + 1) * 128, :], inc="g%d" % (2 + p))
            R.wait("tensor", "dW", w_d)
            R.wait("tensor", "g%d" % p, ga)
            R.wait("tensor", "vE", psfree[p])
            for dg in range(4):
                for ct in range(16):
                    R.op("tensor", lambda e, p=p, dg=dg, ct=ct: e.matmul(
                        PSq[p][:, dg * 512:(dg + 1) * 512],
                        at[p][:, ct * 128:(ct + 1) * 128],
                        wsb[:, ct * 2048 + dg * 512:ct * 2048 + (dg + 1) * 512],
                        start=(ct == 0), stop=False))
                R.op("tensor", lambda e, p=p, dg=dg: e.matmul(
                    PSq[p][:, dg * 512:(dg + 1) * 512], C["ones_rb"][0:1, 0:128],
                    C["bo_r"][0:1, dg * 512:(dg + 1) * 512],
                    start=False, stop=True),
                    inc=("tE" if dg == 3 else None))
            afree[p] = cnt["tE"]
            R.wait("vector", "tE", cnt["tE"])
            R.wait("vector", "g%d" % (2 + p), gx)
            R.wait("vector", "g%d" % (4 + p), ofree[p])
            psfree[p] = xfree[p] = R.op(
                "vector", lambda e, p=p: e.scalar_tensor_tensor(
                    ot[p][:, :], xt[p][:, :], 1.0, PSq[p][:, :],
                    ALU.mult, ALU.add), inc="vE")
            R.wait("gpsimd", "vE", cnt["vE"])
            ofree[p] = R.dma("gpsimd", dr["xbuf"][t * 128:(t + 1) * 128, :],
                             ot[p][:, :], inc="g%d" % (4 + p))
        R.fence()
        R.emit()



def make_inputs(full):
    x = np.ascontiguousarray(full["L0_attn_a_x"])
    W_in = np.zeros((D, PROJ), np.float32)
    W_in[:, :4192] = full["L0_attn_a_W_in"]
    b_in = np.zeros((PROJ,), np.float32)
    b_in[:4192] = full["L0_attn_a_b_in"]
    conv_w = np.zeros((HBCT * 128, 4), np.float32)
    conv_w[:2112] = full["L0_attn_a_conv_w"]
    conv_b = np.zeros((HBCT * 128,), np.float32)
    conv_b[:2112] = full["L0_attn_a_conv_b"]

    com = {
        "W_in": W_in,
        "b_in_cols": b_in.reshape(NPO, 128).T,
        "conv_w_cols": conv_w.reshape(HBCT, 128, 4).transpose(1, 0, 2
                                                              ).reshape(128, HBCT * 4),
        "conv_b_cols": conv_b.reshape(HBCT, 128).T,
        "A_col": full["L0_attn_a_A_log"].reshape(32, 1),
        "dtb_col": full["L0_attn_a_dt_bias"].reshape(32, 1),
        "D_cols": full["L0_attn_a_D_param"].reshape(16, 2)[:, :, None].repeat(
            64, axis=2).reshape(16, 128).T,
        "norm_cols": full["L0_attn_a_norm_w"].reshape(16, 128).T,
        "W_out": full["L0_attn_a_W_out"],
        "Wq": full["L0_attn_b_Wq"], "Wk": full["L0_attn_b_Wk"],
        "Wv": full["L0_attn_b_Wv"], "Wo": full["L0_attn_b_Wo"],
        "bq_cols": full["L0_attn_b_bq"].reshape(16, 128).T,
        "bk_cols": full["L0_attn_b_bk"].reshape(16, 128).T,
        "bv_row": full["L0_attn_b_bv"].reshape(1, D),
        "bo_row": full["L0_attn_b_bo"].reshape(1, D),
        "w_gate": full["L0_ffn_w_gate"], "w_up": full["L0_ffn_w_up"],
        "w_down": full["L0_ffn_w_down"],
    }
    com = {k: np.ascontiguousarray(np.asarray(v, dtype=np.float32))
           for k, v in com.items()}
    com["neg_ones"] = np.full((1, D), -1.0, np.float32)
    com["ident_bf"] = np.eye(128, dtype=ml_dtypes.bfloat16)
    com["ident_f32"] = np.eye(128, dtype=np.float32)
    com["triu_bf"] = np.triu(np.ones((128, 128))).astype(ml_dtypes.bfloat16)
    maps = []
    for c in range(NCORES):
        m = dict(com)
        m["x"] = np.ascontiguousarray(x[c].astype(np.float32, copy=False))
        maps.append(m)
    return maps


def kernel(**inputs):
    maps = make_inputs(inputs)
    nc = build_nc(upto="all")
    res = run_bass_kernel_spmd(nc, maps, core_ids=list(range(NCORES)))
    return np.stack([np.asarray(r["out"], dtype=np.float32)
                     for r in res.results], axis=0)
